# version 25
# baseline (speedup 1.0000x reference)
import math
import numpy as np
import concourse.bass as bass
import concourse.mybir as mybir
from concourse.bass_utils import run_bass_kernel_spmd
from concourse.alu_op_type import AluOpType
from contextlib import ExitStack

F32 = mybir.dt.float32
BF16 = mybir.dt.bfloat16
AF = mybir.ActivationFunctionType
AX = mybir.AxisListType

S = 4096
D = 1024
H = 16
FF = 4096
NBLK = 8
BLK = 512
NQB = 8
QB = 512
NPAIR = 8
NKC = 32
EPS = 1e-5
EXP_SCALE = 2.0 ** -13
EXP_BIAS = -30.0
C_ATTN = (S / math.sqrt(S - 1.0)) * (2.0 ** -16)


class _Sem:
    __slots__ = ("name", "n", "h")

    def __init__(self, name):
        self.name = name
        self.n = 0
        self.h = None


DEBUG_TAPS = False
_B_SERIAL = False


def build_program():
    nc = bass.Bass("TRN2", target_bir_lowering=False)
    _scr = "ExternalOutput" if DEBUG_TAPS else "Internal"

    x0_d = nc.dram_tensor("x0", [S, D], F32, kind="ExternalInput").ap()
    wqkvT_d = nc.dram_tensor("wqkvT", [D, 3 * D], F32, kind="ExternalInput").ap()
    woq_d = nc.dram_tensor("woq", [64, 16384], F32, kind="ExternalInput").ap()
    wfc_d = nc.dram_tensor("wfc", [FF, D], F32, kind="ExternalInput").ap()
    wproj_d = nc.dram_tensor("wproj", [D, FF], F32, kind="ExternalInput").ap()
    vT_d = nc.dram_tensor("vT", [D, 16], F32, kind="ExternalInput").ap()
    fbT_d = nc.dram_tensor("fbT", [16, 16], F32, kind="ExternalInput").ap()
    uT_d = nc.dram_tensor("uT", [16, FF], F32, kind="ExternalInput").ap()
    gate_d = nc.dram_tensor("gate", [1, 1], F32, kind="ExternalInput").ap()
    out_d = nc.dram_tensor("out", [S, D], F32, kind="ExternalOutput").ap()

    xn1_scr = nc.dram_tensor("xn1_scr", [S, D], BF16, kind="Internal").ap()
    q_scr = nc.dram_tensor("q_scr", [D, S], BF16, kind=_scr).ap()
    x1_scr = nc.dram_tensor("x1_scr", [S, D], F32, kind=_scr).ap()
    xn2_scr = nc.dram_tensor("xn2_scr", [S, D], BF16, kind=_scr).ap()
    woq_bf_d = nc.dram_tensor("woq_bf", [64, 16384], BF16, kind="Internal").ap()
    wfct_scr = nc.dram_tensor("wfct_scr", [FF, D], BF16, kind="Internal").ap()
    wprjt_scr = nc.dram_tensor("wprjt_scr", [D, FF], BF16, kind=_scr).ap()
    wfcT_scr = nc.dram_tensor("wfcT_scr", [D, FF], BF16, kind=_scr).ap()
    wprjT_scr = nc.dram_tensor("wprjT_scr", [FF, D], BF16, kind=_scr).ap()
    rec_d = nc.dram_tensor("rec_d", [1, 1024], F32, kind="Internal").ap()
    q_scr_kind = _scr
    attn_dbg = nc.dram_tensor("attn_dbg", [64, 8192], BF16, kind=_scr).ap() if DEBUG_TAPS else None
    proj_dbg = nc.dram_tensor("proj_dbg", [512, 1024], F32, kind=_scr).ap() if DEBUG_TAPS else None
    woq_dbg = nc.dram_tensor("woq_dbg", [64, 8192], BF16, kind=_scr).ap() if DEBUG_TAPS else None
    kT_dbg = nc.dram_tensor("kT_dbg", [128, 32768], BF16, kind=_scr).ap() if DEBUG_TAPS else None
    va_dbg = nc.dram_tensor("va_dbg", [128, 33280], BF16, kind=_scr).ap() if DEBUG_TAPS else None

    events = []
    sems = []

    def sem(name):
        s = _Sem(name)
        sems.append(s)
        return s

    sMM = sem("sMM")
    sEX = sem("sEX")
    sV = sem("sV")
    sIO = sem("sIO")
    sN1 = sem("sN1")
    sX1 = sem("sX1")
    sXN2 = sem("sXN2")
    sQW = sem("sQW")
    sQR = sem("sQR")
    sWL = sem("sWL")
    sTL = sem("sTL")
    sTW = sem("sTW")
    sTT = sem("sTT")
    sT1 = sem("sT1")
    sWB = sem("sWB")
    sOUT = sem("sOUT")
    sT1A = sem("sT1A")
    sT1B = sem("sT1B")
    sXTA = sem("sXTA")
    sXTB = sem("sXTB")
    sQRA = sem("sQRA")
    sQRB = sem("sQRB")
    sBC = sem("sBC")
    sWBA = sem("sWBA")
    sWBB = sem("sWBB")
    sWF = sem("sWF")
    sTLA = sem("sTLA")
    sTLB = sem("sTLB")
    sXLA = sem("sXLA")
    sXLB = sem("sXLB")

    def ev(eng, waits, fn):
        events.append((eng, [(s, v) for (s, v) in waits if v > 0], fn))

    def op(eng, emit1, s, waits=(), k=1):
        s.n += k
        tgt = s.n

        def f(e, emit1=emit1, s=s, k=k):
            emit1(e).then_inc(s.h, k)
        ev(eng, waits, f)
        return tgt

    def dma(q, out, in_, s, waits=(), transpose=False, chain=True):
        prev = s.n
        s.n += 16
        tgt = s.n
        w = list(waits)
        if chain and prev > 0:
            w.append((s, prev))

        def f(e, out=out, in_=in_, s=s, transpose=transpose):
            e.dma_start(out=out, in_=in_, transpose=transpose).then_inc(s.h, 16)
        ev(q, w, f)
        return tgt

    with ExitStack() as stack:
        ent = stack.enter_context
        pool_a = ent(nc.sbuf_tensor("pool_a", [128, 33280], BF16))
        pool_k = ent(nc.sbuf_tensor("pool_k", [128, 32768], BF16))
        pool_w = ent(nc.sbuf_tensor("pool_w", [128, 16384], BF16))
        pool_x32 = ent(nc.sbuf_tensor("pool_x32", [128, 4096], F32))
        pool_x16 = ent(nc.sbuf_tensor("pool_x16", [128, 10240], BF16))
        pool_st = ent(nc.sbuf_tensor("pool_st", [128, 2048], BF16))
        pool_q = ent(nc.sbuf_tensor("pool_q", [128, 1024], BF16))
        pool_m = ent(nc.sbuf_tensor("pool_m", [128, 192], F32))

        for s in sems:
            s.h = ent(nc.semaphore(s.name))

        PS_A = ent(nc.psum_tensor("PS_A", [128, 1024], F32))
        PS_B = ent(nc.psum_tensor("PS_B", [128, 1024], F32))
        PS_C = ent(nc.psum_tensor("PS_C", [128, 1024], F32))
        PS_D = ent(nc.psum_tensor("PS_D", [128, 1024], F32))

        c_exp_bias = pool_m[:, 0:1]
        c_exp_scale = pool_m[:, 1:2]
        c_eps = pool_m[:, 2:3]
        c_mhalf = pool_m[:, 3:4]
        c_mone = pool_m[:, 4:5]
        g16 = pool_m[0:16, 8:9]
        bn6 = pool_m[:, 9:21]
        bn6b = pool_m[:, 21:33]
        stats = pool_m[:, 33:41]
        stats2 = pool_m[:, 41:49]
        lnv = pool_m[:, 49:53]
        lnv2 = pool_m[:, 53:57]
        rstd = pool_m[:, 57:61]
        rstd2 = pool_m[:, 61:65]
        s_col = pool_m[:, 65:66]
        h_col = pool_m[:, 66:67]
        nh_col = pool_m[:, 67:68]
        tpart = pool_m[:, 68:72]
        ones16 = pool_m[0:1, 72:88]
        cones = pool_m[64:65, 88:152]
        gate_sb = pool_m[0:1, 152:153]

        def mset(ap, val):
            def f(e, ap=ap, val=val):
                return e.memset(ap, val)
            return f

        op("vector", mset(c_exp_bias, EXP_BIAS), sV)
        op("vector", mset(c_exp_scale, EXP_SCALE), sV)
        op("vector", mset(c_eps, EPS), sV)
        op("vector", mset(c_mhalf, -0.5), sV)
        op("vector", mset(c_mone, -1.0), sV)
        op("vector", mset(ones16, 1.0), sV)
        op("vector", mset(cones, C_ATTN), sV)
        va_ones = pool_a[:].rearrange("p (c h w) -> p c h w", h=16, w=65)[:, :, :, 64:65]
        cst_done = op("vector", mset(va_ones, 1.0), sV)

        g_load = dma("gpsimd", gate_sb, gate_d, sWL)
        g_mm = op("tensor", lambda e: e.matmul(PS_C[0:16, 0:1], ones16, gate_sb,
                                               start=True, stop=True), sMM,
                  waits=[(sWL, g_load), (sV, cst_done)])
        g_cp = op("vector", lambda e: e.tensor_copy(g16, PS_C[0:16, 0:1]), sV, waits=[(sMM, g_mm)])

        wq_sb = pool_w[:, 0:8192]
        wq_t = dma("gpsimd", wq_sb[:].rearrange("p (c f) -> p c f", f=D),
                   wqkvT_d[:, 0:D].rearrange("(c p) f -> p c f", p=128), sWL)
        woq_stage = pool_k[0:64, 0:16384]
        woq_t = dma("gpsimd", woq_stage, woq_d, sWL)
        woq_w = dma("gpsimd", woq_bf_d, woq_stage, sQW, waits=[(sWL, woq_t)])

        x0blk = pool_x32[:, 0:4096]
        xn1blks = [pool_x16[:, 0:4096], pool_x16[:, 4096:8192]]
        xn1T = [pool_w[:, 8192 + i * 4096: 8192 + (i + 1) * 4096] for i in range(2)]
        qstage = [pool_q[:, i * 512:(i + 1) * 512] for i in range(2)]

        ln_done = [0] * NBLK
        n1_done = [0] * NBLK
        q_mm_done = [0] * NBLK
        q_evac = [0] * (NBLK * 8)
        qw_tgts = [0] * (NBLK * 8)

        n1_prev = [0]
        x0h = [pool_x32[:, 0:2048], pool_x32[:, 2048:4096]]
        half_norm_done = [0] * (2 * NBLK)
        blk_tt = [None] * NBLK

        def emit_ln_half(b, hh):
            xn1blk = xn1blks[b % 2]
            gh = 2 * b + hh
            xw = [(sV, half_norm_done[gh - 2])] if gh >= 2 else [(sV, cst_done)]
            x_t = dma("sync", x0h[gh % 2][:].rearrange("p (t d) -> p t d", d=D),
                      x0_d[b * BLK + hh * 256: b * BLK + (hh + 1) * 256, :]
                      .rearrange("(t p) d -> p t d", p=128),
                      sIO, waits=xw)
            ag_t = 0
            for t2 in range(2):
                t = hh * 2 + t2
                buf = bn6 if t % 2 == 0 else bn6b
                srcx = x0h[gh % 2][:, t2 * 1024:(t2 + 1) * 1024]

                def f_st(e, buf=buf, srcx=srcx):
                    e.bn_stats(buf[:, 0:6], srcx[:, 0:512])
                    return e.bn_stats(buf[:, 6:12], srcx[:, 512:1024])
                st_t = op("vector", f_st, sV, waits=[(sIO, x_t)])
                ag_t = op("vector",
                          (lambda e, buf=buf, t=t: e.bn_aggr(stats[:, 2 * t:2 * t + 2],
                                                             buf[:, 0:12])),
                          sV, waits=[(sV, st_t)])
            ex_t = 0
            for t2 in range(2):
                t = hh * 2 + t2
                l_t = op("scalar", (lambda e, t=t: e.activation(lnv[:, t:t + 1],
                                                                stats[:, 2 * t + 1:2 * t + 2],
                                                                AF.Ln, bias=c_eps)),
                         sEX, waits=[(sV, ag_t)] if t2 == 0 else [])
                ex_t = op("scalar", (lambda e, t=t: e.activation(rstd[:, t:t + 1], lnv[:, t:t + 1],
                                                                 AF.Exp, scale=c_mhalf)),
                          sEX, waits=[(sEX, l_t)])
            nt = 0
            for t2 in range(2):
                t = hh * 2 + t2

                def f_nm(e, t=t, t2=t2, xn1blk=xn1blk, gh=gh):
                    return e.tensor_scalar(xn1blk[:, t * 1024:(t + 1) * 1024],
                                           x0h[gh % 2][:, t2 * 1024:(t2 + 1) * 1024],
                                           stats[:, 2 * t:2 * t + 1], rstd[:, t:t + 1],
                                           AluOpType.subtract, AluOpType.mult)
                nmw = [(sEX, ex_t)] if t2 == 0 else []
                if t2 == 0 and b >= 2 and hh == 0:
                    nmw.append((sN1, n1_done[b - 2]))
                nt = op("vector", f_nm, sV, waits=nmw)
            half_norm_done[gh] = nt
            if hh == 1:
                ln_done[b] = nt
                n1_t = dma("sync",
                           xn1_scr[b * BLK:(b + 1) * BLK, :].rearrange("(t p) d -> p t d", p=128),
                           xn1blk[:].rearrange("p (t d) -> p t d", d=D), sN1, waits=[(sV, nt)])
                n1_done[b] = n1_t
                n1_prev[0] = n1_t
                ring = xn1T[b % 2]
                rsem = [sT1A, sT1B][b % 2]
                trw = [(sN1, n1_t)]
                if b >= 2:
                    trw.append((sMM, q_mm_done[b - 2]))
                tt = 0
                for c in range(8):
                    tt = dma("scalar", ring[:, c * 512:(c + 1) * 512],
                             xn1_scr[b * BLK:(b + 1) * BLK, c * 128:(c + 1) * 128],
                             rsem, waits=trw if c == 0 else [], transpose=True, chain=False)
                blk_tt[b] = (rsem, tt)

        def emit_q_tiles(b, f0, f1):
            ring = xn1T[b % 2]
            rsem, tt = blk_tt[b]
            for f in range(f0, f1):
                idx = b * 8 + f
                bank = [PS_A[:, 0:512], PS_A[:, 512:1024]][idx % 2]

                def f_q(e, ring=ring, f=f, bank=bank):
                    r = None
                    for c in range(8):
                        r = e.matmul(bank, wq_sb[:, c * 1024 + f * 128: c * 1024 + (f + 1) * 128],
                                     ring[:, c * 512:(c + 1) * 512], start=(c == 0), stop=(c == 7))
                    return r
                qw = [(rsem, tt), (sWL, wq_t)]
                if idx >= 2:
                    qw.append((sV, q_evac[idx - 2]))
                qm = op("tensor", f_q, sMM, waits=qw)
                q_mm_done[b] = qm
                qs = qstage[idx % 2]
                qev_w = [(sMM, qm)]
                if idx >= 2:
                    qev_w.append((sQW, qw_tgts[idx - 2]))
                qev = op("vector", (lambda e, bank=bank, qs=qs: e.tensor_copy(qs, bank)),
                         sV, waits=qev_w)
                q_evac[idx] = qev
                qw_tgts[idx] = dma("gpsimd", q_scr[f * 128:(f + 1) * 128, b * BLK:(b + 1) * BLK],
                                   qs, sQW, waits=[(sV, qev)])

        for b in range(NBLK):
            emit_ln_half(b, 0)
            if b >= 1:
                emit_q_tiles(b - 1, 0, 4)
            emit_ln_half(b, 1)
            if b >= 1:
                emit_q_tiles(b - 1, 4, 8)
        emit_q_tiles(NBLK - 1, 0, 8)

        pass1_mm = sMM.n
        pass1_v = sV.n
        last_qw = sQW.n

        wk_sb = pool_w[:, 0:8192]
        wv_sb = pool_w[:, 8192:16384]
        wk_t = dma("gpsimd", wk_sb[:].rearrange("p (c f) -> p c f", f=D),
                   wqkvT_d[:, D:2 * D].rearrange("(c p) f -> p c f", p=128),
                   sWL, waits=[(sMM, pass1_mm), (sV, pass1_v)])
        wv_t = dma("gpsimd", wv_sb[:].rearrange("p (c f) -> p c f", f=D),
                   wqkvT_d[:, 2 * D:3 * D].rearrange("(c p) f -> p c f", p=128),
                   sWL, waits=[(sMM, pass1_mm)])

        xn1T2 = [pool_x16[:, i * 4096:(i + 1) * 4096] for i in range(2)]
        kT = pool_k
        v_aug4 = pool_a[:].rearrange("p (c h w) -> p c h w", h=16, w=65)

        k_evac = [0] * (NBLK * 8)
        v_evac = [0] * (NBLK * 8)
        blk_mm_end = [0] * NBLK

        for b in range(NBLK):
            ring = xn1T2[b % 2]
            rsem = [sT1A, sT1B][b % 2]
            trw = [(sN1, 16 * NBLK)]
            if b == 0:
                trw += [(sWL, wv_t), (sV, pass1_v)]
            if b >= 2:
                trw.append((sMM, blk_mm_end[b - 2]))
            tt = 0
            for c in range(8):
                tt = dma("scalar", ring[:, c * 512:(c + 1) * 512],
                         xn1_scr[b * BLK:(b + 1) * BLK, c * 128:(c + 1) * 128],
                         rsem, waits=trw if c == 0 else [], transpose=True, chain=False)
            for f in range(8):
                idx = b * 8 + f
                bank = [PS_A[:, 0:512], PS_A[:, 512:1024]][idx % 2]

                def f_k(e, ring=ring, f=f, bank=bank):
                    r = None
                    for c in range(8):
                        r = e.matmul(bank, wk_sb[:, c * 1024 + f * 128: c * 1024 + (f + 1) * 128],
                                     ring[:, c * 512:(c + 1) * 512], start=(c == 0), stop=(c == 7))
                    return r
                kw = [(rsem, tt), (sWL, wk_t)]
                if idx >= 2:
                    kw.append((sV, k_evac[idx - 2]))
                km = op("tensor", f_k, sMM, waits=kw)
                k_evac[idx] = op("vector",
                                 (lambda e, bank=bank, b=b, f=f:
                                  e.tensor_copy(kT[:, f * 4096 + b * 512: f * 4096 + (b + 1) * 512], bank)),
                                 sV, waits=[(sMM, km)])
                blk_mm_end[b] = km
            for j in range(4):
                for hf in range(2):
                    idx = b * 8 + j * 2 + hf
                    bank = [PS_B[:, 0:512], PS_B[:, 512:1024]][idx % 2]

                    def f_v(e, ring=ring, j=j, hf=hf, bank=bank):
                        r = None
                        for c in range(8):
                            r = e.matmul(bank, ring[:, c * 512 + j * 128: c * 512 + (j + 1) * 128],
                                         wv_sb[:, c * 1024 + hf * 512: c * 1024 + (hf + 1) * 512],
                                         start=(c == 0), stop=(c == 7))
                        return r
                    vw = [(rsem, tt), (sWL, wv_t)]
                    if idx >= 2:
                        vw.append((sV, v_evac[idx - 2]))
                    vm = op("tensor", f_v, sMM, waits=vw)
                    blk_mm_end[b] = vm
                    cg = b * 4 + j
                    dst = v_aug4[:, cg, hf * 8:(hf + 1) * 8, 0:64]
                    src = bank.rearrange("p (h w) -> p h w", w=64)
                    v_evac[idx] = op("vector", (lambda e, dst=dst, src=src: e.tensor_copy(dst, src)),
                                     sV, waits=[(sMM, vm)])

        a1_mm = sMM.n
        a1_v = sV.n

        st_ring = [pool_st[:, i * 1024:(i + 1) * 1024] for i in range(2)]
        qring = [pool_q[:, i * 512:(i + 1) * 512] for i in range(2)]
        attn_sb = pool_w[0:64, 0:8192]
        woqq = [pool_w[0:64, 8192:12288], pool_w[0:64, 12288:16384]]
        woqr = pool_w[0:64, 8192:16384]
        x0t = pool_x32[:, 0:1024]
        aout = pool_x32[:, 1024:2048]
        x1t = pool_x32[:, 2048:3072]
        bcrec = pool_x32[:, 3072:4096]
        rec = bcrec[64:65, :]
        bc_sb = bcrec[0:64, :]
        xn2ring = [pool_x16[:, i * 1024:(i + 1) * 1024] for i in range(2)]

        pxf = pool_x16.bitcast(F32)
        wfst = [pxf[:, 1024 + i * 512: 1024 + (i + 1) * 512 + 512] for i in range(0)] or \
               [pxf[:, 1024:2048], pxf[:, 2048:3072]]
        stg16 = [pool_x16[:, 2048 + i * 2048: 2048 + (i + 1) * 2048] for i in range(2)]
        tern_a = pool_x16[:, 6144:7168]
        tern_b = pool_x16[:, 7168:8192]
        t1ring = [pool_x16[:, 8192 + i * 1024: 8192 + (i + 1) * 1024] for i in range(2)]

        slot_w_guard = [0, 0]
        slot_v_guard = [0, 0]
        last_comb = {"n": 0}
        tern_w_done = {"n": 0}

        def tern_compare_chain(j, dst_dram):
            g1 = op("vector", (lambda e, j=j: e.tensor_single_scalar(tern_a, wfst[j], h_col[:, 0:1],
                                                                     AluOpType.is_ge)),
                    sV, waits=[(sV, last_comb["n"])])
            g2 = op("vector", (lambda e, j=j: e.tensor_single_scalar(tern_b, wfst[j], nh_col[:, 0:1],
                                                                     AluOpType.is_le)),
                    sV, waits=[(sV, g1)])
            c1 = op("vector", (lambda e, j=j: e.tensor_sub(stg16[j][:, 0:1024], tern_a, tern_b)),
                    sV, waits=[(sV, g2)])
            c2 = op("vector", (lambda e, j=j: e.tensor_scalar_mul(stg16[j][:, 1024:2048],
                                                                  stg16[j][:, 0:1024], s_col[:, 0:1])),
                    sV, waits=[(sV, c1)])
            wt = dma("gpsimd", dst_dram, stg16[j][:, 1024:2048], sTW, waits=[(sV, c2)])
            last_comb["n"] = c2
            slot_w_guard[j] = wt
            slot_v_guard[j] = c2
            return wt

        def stage_load(j, src):
            w = []
            if slot_v_guard[j]:
                w.append((sV, slot_v_guard[j]))
            if slot_w_guard[j]:
                w.append((sTW, slot_w_guard[j]))
            return dma("gpsimd", wfst[j], src, [sTLA, sTLB][j], waits=w, chain=False)

        tern_seq = {"i": 0}

        def emit_tern_fc(i):
            j = i % 2
            lt = stage_load(j, wfc_d[i * 128:(i + 1) * 128, :])
            r1 = op("vector", (lambda e, j=j: e.reduce_sum(s_col, wfst[j], AX.X,
                                                           apply_absolute_value=True)),
                    sV, waits=[([sTLA, sTLB][j], lt)])
            r2 = op("vector", lambda e: e.tensor_scalar(s_col, s_col, 1.0 / D, 1e-5,
                                                        AluOpType.mult, AluOpType.max),
                    sV, waits=[(sV, r1)])
            r3 = op("vector", lambda e: e.tensor_scalar_mul(h_col, s_col, 0.5), sV, waits=[(sV, r2)])
            r4 = op("vector", lambda e: e.tensor_scalar_mul(nh_col, s_col, -0.5), sV, waits=[(sV, r3)])
            last_comb["n"] = max(last_comb["n"], r4)
            slot_v_guard[j] = max(slot_v_guard[j], r4)
            tern_w_done["n"] = tern_compare_chain(j, wfct_scr[i * 128:(i + 1) * 128, :])

        def emit_tern_proj(i):
            ch_last = 0
            for c in range(4):
                j = tern_seq["i"] % 2
                tern_seq["i"] += 1
                lt = stage_load(j, wproj_d[i * 128:(i + 1) * 128, c * 1024:(c + 1) * 1024])
                ch_last = op("vector", (lambda e, j=j, c=c: e.reduce_sum(tpart[:, c:c + 1], wfst[j], AX.X,
                                                                        apply_absolute_value=True)),
                             sV, waits=[([sTLA, sTLB][j], lt)])
                slot_v_guard[j] = ch_last
            r1 = op("vector", lambda e: e.reduce_sum(s_col, tpart, AX.X), sV, waits=[(sV, ch_last)])
            r2 = op("vector", lambda e: e.tensor_scalar(s_col, s_col, 1.0 / FF, 1e-5,
                                                        AluOpType.mult, AluOpType.max),
                    sV, waits=[(sV, r1)])
            r3 = op("vector", lambda e: e.tensor_scalar_mul(h_col, s_col, 0.5), sV, waits=[(sV, r2)])
            r4 = op("vector", lambda e: e.tensor_scalar_mul(nh_col, s_col, -0.5), sV, waits=[(sV, r3)])
            last_comb["n"] = max(last_comb["n"], r4)
            for c in range(4):
                j = tern_seq["i"] % 2
                tern_seq["i"] += 1
                lt = stage_load(j, wproj_d[i * 128:(i + 1) * 128, c * 1024:(c + 1) * 1024])
                g_pre = op("vector", (lambda e: e.engine_nop()), sV, waits=[([sTLA, sTLB][j], lt)])
                last_comb["n"] = max(last_comb["n"], g_pre)
                tern_w_done["n"] = tern_compare_chain(
                    j, wprjt_scr[i * 128:(i + 1) * 128, c * 1024:(c + 1) * 1024])

        tern_units = [("fc", i) for i in range(32)] + [("proj", i) for i in range(8)]
        tern_iter = iter(tern_units)

        t1_units = [("fc", a, fb) for a in range(8) for fb in range(8)] + \
                   [("proj", a, 0) for a in range(32)]
        t1_iter = iter(t1_units)
        t1_state = {"cnt": 0, "busy": [0, 0]}

        def emit_t1_unit(u):
            kind, a, bnum = u
            j = t1_state["cnt"] % 2
            t1_state["cnt"] += 1
            w = [(sTW, tern_w_done["n"])]
            if t1_state["busy"][j]:
                w.append((sTT, t1_state["busy"][j]))
            tsem = [sT1A, sT1B][j]
            if kind == "fc":
                rt = dma("sync", t1ring[j][:, 0:512],
                         wfct_scr[bnum * 512:(bnum + 1) * 512, a * 128:(a + 1) * 128],
                         tsem, waits=w, transpose=True, chain=False)
                wt = dma("gpsimd", wfcT_scr[a * 128:(a + 1) * 128, bnum * 512:(bnum + 1) * 512],
                         t1ring[j][:, 0:512], sTT, waits=[(tsem, rt)])
            else:
                rt = dma("sync", t1ring[j],
                         wprjt_scr[0:D, a * 128:(a + 1) * 128],
                         tsem, waits=w, transpose=True, chain=False)
                wt = dma("gpsimd", wprjT_scr[a * 128:(a + 1) * 128, :], t1ring[j],
                         sTT, waits=[(tsem, rt)])
            t1_state["busy"][j] = wt

        def emit_side_work(group_idx):
            if group_idx < 40:
                u = next(tern_iter, None)
                if u is not None:
                    if u[0] == "fc":
                        emit_tern_fc(u[1])
                    else:
                        emit_tern_proj(u[1])
            else:
                for _ in range(14):
                    u = next(t1_iter, None)
                    if u is not None:
                        emit_t1_unit(u)

        if DEBUG_TAPS:
            dma("sync", kT_dbg, kT[:, 0:32768], sIO, waits=[(sV, a1_v), (sMM, a1_mm)])
            dma("sync", va_dbg, pool_a[:, 0:33280], sIO, waits=[(sV, a1_v), (sMM, a1_mm)])

        x1t65 = pool_x32[0:65, 2048:3072]
        x1t64 = pool_x32[0:64, 2048:3072]
        REGS = [PS_A, PS_B, PS_D]
        q_tgt = [0] * NPAIR
        pair_evac_done = a1_v
        ps_c_free = a1_v
        prev_exp_last3 = [0, 0, 0]
        prev_exp_psD = 0
        last_proj_mm = a1_mm
        psD_free_v = a1_v
        last_scores_mm = 0
        last_attnv_mm = 0
        prev_ln2_sc = 0
        ln2_prev_qb = 0
        prev_add1 = 0
        x1_written = [0] * NQB
        xn2_written = [0] * NQB
        xn2w_slot = [0, 0]
        woq_slot_free = a1_mm
        woq_slot_free_b = a1_mm
        woqr_pre = [0, 0]
        reg_ln_free = [0, 0, 0, 0]
        group = 0
        side_cnt = [0]
        prev_rcw = [0]
        prev_bcd = [0]
        prev_x1w = [0]

        qrd_pending = [0, 0]
        for qb in range(NQB):
            for p in range(NPAIR):
                if qb == 0 and p == 0:
                    qrd_pending[0] = dma("gpsimd", qring[0], q_scr[0:128, 0:QB], sQRA,
                                         waits=[(sQW, last_qw), (sV, a1_v)], chain=False)
                    qrd_pending[1] = dma("gpsimd", qring[1], q_scr[128:256, 0:QB], sQRB,
                                         waits=[(sQW, last_qw)], chain=False)
                qsem = [sQRA, sQRB][p % 2]
                qrd = qrd_pending[p % 2]

                exp_tgt = [0] * NKC
                attnv_tgt = [0] * NKC
                sc_tgt = [0] * NKC

                def emit_scores(kc, p=p, qb=qb, qrd=qrd, sc_tgt=sc_tgt, exp_tgt=exp_tgt):
                    REG = REGS[kc % 3]
                    w = []
                    if kc >= 3:
                        w.append((sEX, exp_tgt[kc - 3]))
                    else:
                        w.append((sEX, prev_exp_last3[kc]))
                        if p == 0 and reg_ln_free[kc]:
                            w.append((sV, reg_ln_free[kc if kc < 2 else 3]))
                    if kc == 0:
                        w.append((qsem, qrd))
                        if qb == 0 and p == 0:
                            w.append((sV, a1_v))
                            w.append((sMM, a1_mm))

                    def f(e, REG=REG, p=p, kc=kc):
                        e.matmul(REG[:, 0:512],
                                 kT[0:64, p * 4096 + kc * 128: p * 4096 + (kc + 1) * 128],
                                 qring[p % 2][0:64, :], start=True, stop=True, tile_position=(0, 0))
                        return e.matmul(REG[:, 512:1024],
                                        kT[64:128, p * 4096 + kc * 128: p * 4096 + (kc + 1) * 128],
                                        qring[p % 2][64:128, :], start=True, stop=True,
                                        tile_position=(64, 0))
                    sc_tgt[kc] = op("tensor", f, sMM, waits=w)

                def emit_exp(kc, sc_tgt=sc_tgt, attnv_tgt=attnv_tgt, exp_tgt=exp_tgt):
                    REG = REGS[kc % 3]
                    w = [(sMM, sc_tgt[kc])]
                    if kc >= 2:
                        w.append((sMM, attnv_tgt[kc - 2]))

                    def f(e, REG=REG, kc=kc):
                        return e.activation(st_ring[kc % 2], REG[:, 0:1024], AF.Exp,
                                            bias=c_exp_bias, scale=c_exp_scale)
                    exp_tgt[kc] = op("scalar", f, sEX, waits=w)

                def emit_attnv(kc, p=p, exp_tgt=exp_tgt, attnv_tgt=attnv_tgt):
                    w = [(sEX, exp_tgt[kc])]
                    if kc == 0:
                        w.append((sV, ps_c_free))

                    def f(e, p=p, kc=kc):
                        e.matmul(PS_C[0:65, 0:512], v_aug4[:, kc, 2 * p, :],
                                 st_ring[kc % 2][:, 0:512], start=(kc == 0), stop=(kc == NKC - 1))
                        return e.matmul(PS_C[0:65, 512:1024], v_aug4[:, kc, 2 * p + 1, :],
                                        st_ring[kc % 2][:, 512:1024],
                                        start=(kc == 0), stop=(kc == NKC - 1))
                    attnv_tgt[kc] = op("tensor", f, sMM, waits=w)

                for kc in range(NKC):
                    emit_scores(kc)
                    if kc >= 1:
                        emit_attnv(kc - 1)
                    emit_exp(kc)
                emit_attnv(NKC - 1)
                q_tgt[p] = sc_tgt[NKC - 1]
                last_scores_mm = sc_tgt[NKC - 1]
                last_attnv_mm = attnv_tgt[NKC - 1]
                prev_exp_last3 = [exp_tgt[NKC - 2], exp_tgt[NKC - 1], exp_tgt[NKC - 3]]

                cw = [(sMM, attnv_tgt[NKC - 1])]
                if p == 0:
                    cw.append((sV, prev_add1))
                vcp_att = op("vector", lambda e: e.tensor_copy(x1t65, PS_C[0:65, 0:1024]),
                             sV, waits=cw)
                ps_c_free = vcp_att
                if p == NPAIR - 1:
                    lw = [(sV, vcp_att)]
                    if prev_rcw[0]:
                        lw.append((sBC, prev_rcw[0]))
                    lnr = op("scalar", lambda e: e.activation(aout[64:65, 0:1024],
                                                              x1t65[64:65, 0:1024], AF.Ln),
                             sEX, waits=lw)
                    vr = op("scalar", lambda e: e.activation(rec, aout[64:65, 0:1024],
                                                             AF.Exp, scale=c_mone[64:65, 0:1]),
                            sEX, waits=[(sEX, lnr)])
                    rcw_w = [(sEX, vr)]
                else:
                    vr_w = [(sV, vcp_att)]
                    if prev_rcw[0]:
                        vr_w.append((sBC, prev_rcw[0]))
                    vr = op("vector", lambda e: e.reciprocal(rec, x1t65[64:65, 0:1024]),
                            sV, waits=vr_w)
                    rcw_w = [(sV, vr)]
                if prev_bcd[0]:
                    rcw_w.append((sBC, prev_bcd[0]))
                rcw = dma("gpsimd", rec_d, rec, sBC, waits=rcw_w)
                prev_rcw[0] = rcw
                bcd = dma("gpsimd", bc_sb, rec_d.broadcast_to([64, 1024]), sBC,
                          waits=[(sBC, rcw)])
                prev_bcd[0] = bcd
                va1_w = [(sBC, bcd)]
                if p == 0:
                    va1_w.append((sMM, last_proj_mm))
                va1 = op("vector",
                         (lambda e, p=p: e.scalar_tensor_tensor(
                             attn_sb[:, (2 * p) * 512:(2 * p + 1) * 512],
                             x1t64[:, 0:512], C_ATTN, bc_sb[:, 0:512],
                             AluOpType.mult, AluOpType.mult)),
                         sV, waits=va1_w)
                va2 = op("vector",
                         (lambda e, p=p: e.scalar_tensor_tensor(
                             attn_sb[:, (2 * p + 1) * 512:(2 * p + 2) * 512],
                             x1t64[:, 512:1024], C_ATTN, bc_sb[:, 512:1024],
                             AluOpType.mult, AluOpType.mult)),
                         sV, waits=[(sV, va1)])
                pair_evac_done = va2
                prev_exp_psD = exp_tgt[29]

                g2 = qb * NPAIR + p + 2
                if g2 < NQB * NPAIR:
                    qb2, p2 = divmod(g2, NPAIR)
                    qrd_pending[g2 % 2] = dma(
                        "gpsimd", qring[g2 % 2],
                        q_scr[p2 * 128:(p2 + 1) * 128, qb2 * QB:(qb2 + 1) * QB],
                        [sQRA, sQRB][g2 % 2], waits=[(sMM, q_tgt[p])], chain=False)

                if group % 8 <= 5:
                    emit_side_work(side_cnt[0])
                    side_cnt[0] += 1
                if p == 6:
                    woqr_pre[0] = dma("sync", woqq[0], woq_bf_d[:, 0:4096], sWBA,
                                      waits=[(sMM, woq_slot_free)] +
                                      ([(sQW, woq_w)] if qb == 0 else []), chain=False)
                    woqr_pre[1] = dma("sync", woqq[1], woq_bf_d[:, 4096:8192], sWBB,
                                      waits=[(sMM, woq_slot_free_b)] +
                                      ([(sQW, woq_w)] if qb == 0 else []), chain=False)
                group += 1

            if DEBUG_TAPS and qb == NQB - 1:
                dma("sync", attn_dbg, attn_sb, sIO, waits=[(sV, pair_evac_done)])
            qtB2 = [PS_A, PS_B, PS_C, PS_D]
            bank_guards = [[(sEX, prev_exp_last3[0])], [(sEX, prev_exp_last3[1])],
                           [(sV, ps_c_free)], [(sEX, prev_exp_psD)]]
            proj_done = [0] * 4
            qdone = [0, 0, 0, 0]
            for g in range(4):
                nh, hh = g // 2, g % 2
                gsem = [sWBA, sWBB][g % 2]
                if g < 2:
                    wl = woqr_pre[g]
                else:
                    wl = dma("sync", woqq[g % 2], woq_bf_d[:, g * 4096:(g + 1) * 4096],
                             gsem, waits=[(sMM, qdone[g - 2])], chain=False)
                for qt in range(4):
                    def f_pr(e, qt=qt, nh=nh, hh=hh, g=g):
                        r = None
                        for h8 in range(8):
                            h = hh * 8 + h8
                            r = e.matmul(qtB2[qt][:, nh * 512:(nh + 1) * 512],
                                         attn_sb[:, h * 512 + qt * 128: h * 512 + (qt + 1) * 128],
                                         woqq[g % 2][:, h8 * 512:(h8 + 1) * 512],
                                         start=(hh == 0 and h8 == 0),
                                         stop=(hh == 1 and h8 == 7))
                        return r
                    pw = [(gsem, wl)]
                    if g == 0:
                        pw.append((sV, pair_evac_done))
                        pw += bank_guards[qt]
                    proj_done[qt] = op("tensor", f_pr, sMM, waits=pw)
                    last_proj_mm = proj_done[qt]
                qdone[g] = proj_done[3]
            woq_slot_free = qdone[2]
            woq_slot_free_b = qdone[3]
            nm1_tgt = [0] * 4
            for qt in range(4):
                xw = [(sV, prev_add1)] if prev_add1 else []
                x_t = dma("sync", x0t, x0_d[qb * QB + qt * 128: qb * QB + (qt + 1) * 128, :],
                          sIO, waits=xw)

                def f_st1(e, qt=qt):
                    e.bn_stats(bn6[:, 0:6], qtB2[qt][:, 0:512])
                    return e.bn_stats(bn6[:, 6:12], qtB2[qt][:, 512:1024])
                st1 = op("vector", f_st1, sV, waits=[(sMM, proj_done[qt]), (sEX, ln2_prev_qb)])
                ag1 = op("vector", (lambda e, qt=qt: e.bn_aggr(stats[:, 2 * qt:2 * qt + 2], bn6[:, 0:12])),
                         sV, waits=[(sV, st1)])
                l1 = op("scalar", (lambda e, qt=qt: e.activation(lnv[:, qt:qt + 1],
                                                                 stats[:, 2 * qt + 1:2 * qt + 2],
                                                                 AF.Ln, bias=c_eps)),
                        sEX, waits=[(sV, ag1)])
                e1 = op("scalar", (lambda e, qt=qt: e.activation(rstd[:, qt:qt + 1], lnv[:, qt:qt + 1],
                                                                 AF.Exp, scale=c_mhalf)),
                        sEX, waits=[(sEX, l1)])
                nm1 = op("vector", (lambda e, qt=qt: e.tensor_scalar(x1t, qtB2[qt][:, 0:1024],
                                                                    stats[:, 2 * qt:2 * qt + 1],
                                                                    rstd[:, qt:qt + 1],
                                                                    AluOpType.subtract, AluOpType.mult)),
                         sV, waits=[(sEX, e1)])
                nm1_tgt[qt] = nm1
                a1w = [(sV, nm1), (sIO, x_t)]
                if prev_x1w[0]:
                    a1w.append((sX1, prev_x1w[0]))
                add1 = op("vector", lambda e: e.tensor_add(aout, x1t, x0t),
                          sV, waits=a1w)
                prev_add1 = add1
                prev_x1w[0] = dma("sync", x1_scr[qb * QB + qt * 128: qb * QB + (qt + 1) * 128, :],
                                  aout, sX1, waits=[(sV, add1)])

                def f_st2(e):
                    e.bn_stats(bn6b[:, 0:6], aout[:, 0:512])
                    return e.bn_stats(bn6b[:, 6:12], aout[:, 512:1024])
                st2 = op("vector", f_st2, sV, waits=[(sV, add1)])
                ag2 = op("vector", (lambda e, qt=qt: e.bn_aggr(stats2[:, 2 * qt:2 * qt + 2], bn6b[:, 0:12])),
                         sV, waits=[(sV, st2)])
                l2 = op("scalar", (lambda e, qt=qt: e.activation(lnv2[:, qt:qt + 1],
                                                                 stats2[:, 2 * qt + 1:2 * qt + 2],
                                                                 AF.Ln, bias=c_eps)),
                        sEX, waits=[(sV, ag2)])
                e2 = op("scalar", (lambda e, qt=qt: e.activation(rstd2[:, qt:qt + 1], lnv2[:, qt:qt + 1],
                                                                 AF.Exp, scale=c_mhalf)),
                        sEX, waits=[(sEX, l2)])
                prev_ln2_sc = e2
                nm2 = op("vector", (lambda e, qt=qt: e.tensor_scalar(xn2ring[qt % 2], aout,
                                                                    stats2[:, 2 * qt:2 * qt + 1],
                                                                    rstd2[:, qt:qt + 1],
                                                                    AluOpType.subtract, AluOpType.mult)),
                         sV, waits=[(sEX, e2), (sXN2, xn2w_slot[qt % 2])])
                xn2w_slot[qt % 2] = dma("sync",
                                        xn2_scr[qb * QB + qt * 128: qb * QB + (qt + 1) * 128, :],
                                        xn2ring[qt % 2], sXN2, waits=[(sV, nm2)])
            x1_written[qb] = sX1.n
            xn2_written[qb] = sXN2.n
            ln2_prev_qb = prev_ln2_sc
            ps_c_free = nm1_tgt[2]
            psD_free_v = nm1_tgt[3]
            reg_ln_free = nm1_tgt

        a2_mm = sMM.n
        a2_v = sV.n
        a2_sc = sEX.n
        a2_bc = sBC.n

        while True:
            u = next(t1_iter, None)
            if u is None:
                break
            emit_t1_unit(u)
        t1_all = sTT.n

        wfcT = pool_k
        wfc_ld = 0
        for hc in range(8):
            w0 = [(sTT, t1_all), (sMM, last_scores_mm)] if hc == 0 else []
            wfc_ld = dma("gpsimd", wfcT[:, hc * 4096:(hc + 1) * 4096],
                         wfcT_scr[hc * 128:(hc + 1) * 128, :], sWF, waits=w0)
        uT_bf = pool_a[0:16, 0:4096]
        vT_bf = pool_a[:, 4096:4224]
        fbT_bf = pool_a[0:16, 4224:4240]
        dma("gpsimd", uT_bf, uT_d, sWB, waits=[(sMM, last_attnv_mm)])
        dma("gpsimd", vT_bf.rearrange("p (c r) -> p c r", r=16),
            vT_d.rearrange("(c p) r -> p c r", p=128), sWB)
        b_w_ready = dma("gpsimd", fbT_bf, fbT_d, sWB)

        xn2T = [pool_x16[:, i * 4096:(i + 1) * 4096] for i in range(2)]
        wprj_ring = [pool_x16[:, 8192 + i * 1024: 8192 + (i + 1) * 1024] for i in range(2)]
        x1r = [pool_x32[:, i * 1024:(i + 1) * 1024] for i in range(2)]
        ln_b = pool_x32[:, 2048:3072]
        x2_b = pool_x32[:, 3072:4096]
        relu_ring = [pool_st[:, i * 512:(i + 1) * 512] for i in range(2)]
        r2_sb = pool_w
        t_sb = pool_q[0:16, 0:512]
        t2_sb = pool_q[0:16, 512:1024]

        fc_tile_done = [0] * 32
        b_fc_end = [0] * NBLK
        b_ln_prev = a2_sc
        fc_bank_free = [a2_v] * 4
        qt_bank_free_v = [a2_v] * 4
        prev_block_mm = a2_mm
        prev_norm = 0
        x2_stored = 0
        x1_slot = [0, 0]
        wprj_free = [a2_mm, a2_mm]
        fcB = [PS_C[:, 0:512], PS_C[:, 512:1024], PS_A[:, 0:512], PS_A[:, 512:1024]]
        qtB = [PS_A, PS_B, PS_C, PS_D]

        b_tr = [None] * NBLK

        def emit_tr(bb, extra):
            xsem = [sXTA, sXTB][bb % 2]
            trw = [(sXN2, xn2_written[bb])] + extra
            tt = 0
            for c in range(8):
                tt = dma("scalar", xn2T[bb % 2][:, c * 512:(c + 1) * 512],
                         xn2_scr[bb * BLK:(bb + 1) * BLK, c * 128:(c + 1) * 128],
                         xsem, waits=trw if c == 0 else [], transpose=True, chain=False)
            b_tr[bb] = (xsem, tt)

        t2ev_of = [0] * NBLK

        def emit_tchain(bb, extra):
            xsem2, tt2 = b_tr[bb]

            def f_t(e, bb=bb):
                r = None
                for c in range(8):
                    r = e.matmul(PS_C[0:16, 0:512], vT_bf[:, c * 16:(c + 1) * 16],
                                 xn2T[bb % 2][:, c * 512:(c + 1) * 512],
                                 start=(c == 0), stop=(c == 7))
                return r
            tw = [(xsem2, tt2), (sWB, b_w_ready), (sV, qt_bank_free_v[2])] + extra
            t_mm = op("tensor", f_t, sMM, waits=tw)
            t_ev = op("vector", lambda e: e.tensor_scalar_mul(t_sb, PS_C[0:16, 0:512], g16[:, 0:1]),
                      sV, waits=[(sMM, t_mm), (sV, g_cp)])
            t2_mm = op("tensor", lambda e: e.matmul(PS_C[0:16, 512:1024], fbT_bf, t_sb,
                                                    start=True, stop=True), sMM, waits=[(sV, t_ev)])
            t2ev_of[bb] = op("vector", lambda e: e.tensor_copy(t2_sb, PS_C[0:16, 512:1024]),
                             sV, waits=[(sMM, t2_mm)])

        emit_tr(0, [(sXN2, xn2_written[NQB - 1]), (sWF, wfc_ld)])
        for b in range(NBLK):
            xsem, tt = b_tr[b]

            if b == 0:
                emit_tchain(0, [(sV, a2_v)])
            t2_ev = t2ev_of[b]

            for f in range(32):
                bank = fcB[f % 4]

                def f_fc(e, b=b, f=f, bank=bank):
                    for c in range(8):
                        e.matmul(bank, wfcT[:, c * 4096 + f * 128: c * 4096 + (f + 1) * 128],
                                 xn2T[b % 2][:, c * 512:(c + 1) * 512], start=(c == 0), stop=False)
                    return e.matmul(bank, uT_bf[:, f * 128:(f + 1) * 128], t2_sb,
                                    start=False, stop=True)
                fw = [(xsem, tt), (sV, t2_ev), (sV, fc_bank_free[f % 4])]
                if b == 0 and f < 4:
                    fw.append((sEX, a2_sc))
                if f < 4:
                    fw.append((sV, qt_bank_free_v[2 if f < 2 else 0]))
                fcm = op("tensor", f_fc, sMM, waits=fw)
                b_fc_end[b] = fcm
                rl_w = [(sMM, fcm)]
                if fc_tile_done[f - 2]:
                    rl_w.append((sV, fc_tile_done[f - 2]))
                rl = op("scalar", (lambda e, bank=bank, f=f: e.activation(relu_ring[f % 2], bank,
                                                                          AF.Relu)),
                        sEX, waits=rl_w)
                sq = op("vector", (lambda e, f=f: e.tensor_mul(r2_sb[:, f * 512:(f + 1) * 512],
                                                               relu_ring[f % 2], relu_ring[f % 2])),
                        sV, waits=[(sEX, rl)])
                fc_tile_done[f] = sq
                fc_bank_free[f % 4] = sq

            if b + 1 < NBLK:
                if b + 1 == 1:
                    emit_tr(1, [(sXN2, xn2_written[NQB - 1]), (sTT, t1_all)])
                else:
                    emit_tr(b + 1, [(sMM, b_fc_end[b - 1])])

            proj_tgt = 0
            for hc in range(32):
                j = hc % 2
                wl_w = [(sMM, wprj_free[j])]
                if b == 0 and hc < 2:
                    wl_w.append((sTT, t1_all))
                bsem = [sWBA, sWBB][j]
                wl = dma("gpsimd", wprj_ring[j], wprjT_scr[hc * 128:(hc + 1) * 128, :],
                         bsem, waits=wl_w, chain=False)

                def f_pj(e, hc=hc, j=j):
                    r = None
                    for qt in range(4):
                        for nh in range(2):
                            r = e.matmul(qtB[qt][:, nh * 512:(nh + 1) * 512],
                                         r2_sb[:, hc * 512 + qt * 128: hc * 512 + (qt + 1) * 128],
                                         wprj_ring[j][:, nh * 512:(nh + 1) * 512],
                                         start=(hc == 0), stop=(hc == 31))
                    return r
                pw = [(bsem, wl), (sV, fc_tile_done[hc])]
                if hc == 0:
                    pw += [(sV, fc_bank_free[0]), (sV, fc_bank_free[1]),
                           (sV, fc_bank_free[2]), (sV, fc_bank_free[3]),
                           (sV, qt_bank_free_v[1]), (sV, qt_bank_free_v[3])]
                proj_tgt = op("tensor", f_pj, sMM, waits=pw)
                wprj_free[j] = proj_tgt
            prev_block_mm = proj_tgt

            for qt in (2, 0, 1, 3):
                x1w = [(sX1, x1_written[b])]
                if x1_slot[qt % 2]:
                    x1w.append((sV, x1_slot[qt % 2]))
                else:
                    x1w += [(sX1, x1_written[NQB - 1]), (sXN2, xn2_written[NQB - 1]),
                            (sBC, a2_bc)]
                lsem = [sXLA, sXLB][qt % 2]
                x1t_b = dma("sync", x1r[qt % 2],
                            x1_scr[b * BLK + qt * 128: b * BLK + (qt + 1) * 128, :],
                            lsem, waits=x1w, chain=False)

                def f_st(e, qt=qt):
                    e.bn_stats(bn6[:, 0:6], qtB[qt][:, 0:512])
                    return e.bn_stats(bn6[:, 6:12], qtB[qt][:, 512:1024])
                st = op("vector", f_st, sV, waits=[(sMM, proj_tgt), (sEX, b_ln_prev)])
                ag = op("vector", (lambda e, qt=qt: e.bn_aggr(stats[:, 2 * qt:2 * qt + 2], bn6[:, 0:12])),
                        sV, waits=[(sV, st)])
                l1 = op("scalar", (lambda e, qt=qt: e.activation(lnv[:, qt:qt + 1],
                                                                 stats[:, 2 * qt + 1:2 * qt + 2],
                                                                 AF.Ln, bias=c_eps)),
                        sEX, waits=[(sV, ag)])
                e1 = op("scalar", (lambda e, qt=qt: e.activation(rstd[:, qt:qt + 1], lnv[:, qt:qt + 1],
                                                                 AF.Exp, scale=c_mhalf)),
                        sEX, waits=[(sEX, l1)])
                prev_ln2_sc = e1
                nm = op("vector", (lambda e, qt=qt: e.tensor_scalar(ln_b, qtB[qt][:, 0:1024],
                                                                   stats[:, 2 * qt:2 * qt + 1],
                                                                   rstd[:, qt:qt + 1],
                                                                   AluOpType.subtract, AluOpType.mult)),
                        sV, waits=[(sEX, e1)])
                qt_bank_free_v[qt] = nm
                ad_w = [(sV, nm), (lsem, x1t_b)]
                if b == 0:
                    ad_w.append((sBC, a2_bc))
                if x2_stored:
                    ad_w.append((sOUT, x2_stored))
                ad = op("vector", (lambda e, qt=qt: e.tensor_add(x2_b, ln_b, x1r[qt % 2])),
                        sV, waits=ad_w)
                prev_norm = ad
                x1_slot[qt % 2] = ad
                x2_stored = dma("sync", out_d[b * BLK + qt * 128: b * BLK + (qt + 1) * 128, :], x2_b,
                                sOUT, waits=[(sV, ad)])
                if qt == 2 and b + 1 < NBLK:
                    emit_tchain(b + 1, [])
            b_ln_prev = prev_ln2_sc

        ev("sync", [(sOUT, sOUT.n)], lambda e: e.nop())

        with nc.Block() as block:
            def runner(name):
                def _run(e):
                    for (eng, waits, fn) in events:
                        if eng != name:
                            continue
                        for (s, v) in waits:
                            e.wait_ge(s.h, v)
                        fn(e)
                return _run

            block.sync(runner("sync"))
            block.tensor(runner("tensor"))
            block.scalar(runner("scalar"))
            block.vector(runner("vector"))
            block.gpsimd(runner("gpsimd"))

    return nc


_NC_CACHE = {}


def _get_nc():
    if "nc" not in _NC_CACHE:
        _NC_CACHE["nc"] = build_program()
    return _NC_CACHE["nc"]


def kernel(L0_attn_x, L0_attn_w_qkv, L0_attn_w_o, L0_ffn_fast_b, L0_ffn_w_fc,
           L0_ffn_w_proj, L0_ffn_u, L0_ffn_v, L0_ffn_gate):
    x = np.ascontiguousarray(np.asarray(L0_attn_x, dtype=np.float32))
    wqkvT = np.ascontiguousarray(np.asarray(L0_attn_w_qkv, np.float32).T)
    wo_t = np.asarray(L0_attn_w_o, np.float32).T
    woq = np.empty((64, 16384), np.float32)
    for nh in range(2):
        for h in range(16):
            woq[:, nh * 8192 + h * 512: nh * 8192 + (h + 1) * 512] = \
                wo_t[h * 64:(h + 1) * 64, nh * 512:(nh + 1) * 512]
    woq = np.ascontiguousarray(woq)
    wfc = np.ascontiguousarray(np.asarray(L0_ffn_w_fc, np.float32))
    wproj = np.ascontiguousarray(np.asarray(L0_ffn_w_proj, np.float32))
    vT = np.ascontiguousarray(np.asarray(L0_ffn_v, np.float32).T)
    fbT = np.ascontiguousarray(np.asarray(L0_ffn_fast_b, np.float32).T)
    uT = np.ascontiguousarray(np.asarray(L0_ffn_u, np.float32).T)
    gate = np.ascontiguousarray(np.asarray(L0_ffn_gate, np.float32).reshape(1, 1))

    nc = _get_nc()
    shared = {"wqkvT": wqkvT, "woq": woq, "wfc": wfc, "wproj": wproj,
              "vT": vT, "fbT": fbT, "uT": uT, "gate": gate}
    in_maps = [dict(shared, x0=np.ascontiguousarray(x[c])) for c in range(8)]
    res = run_bass_kernel_spmd(nc, in_maps, core_ids=list(range(8)))
    return np.stack([np.asarray(r["out"], np.float32) for r in res.results], axis=0)


if __name__ == "__main__":
    nc = build_program()
    n = 0
    for _ in nc.all_instructions():
        n += 1
    print("instructions:", n)


# version 26
# speedup vs baseline: 1.0005x; 1.0005x over previous
import math
import numpy as np
import concourse.bass as bass
import concourse.mybir as mybir
from concourse.bass_utils import run_bass_kernel_spmd
from concourse.alu_op_type import AluOpType
from contextlib import ExitStack

F32 = mybir.dt.float32
BF16 = mybir.dt.bfloat16
AF = mybir.ActivationFunctionType
AX = mybir.AxisListType

S = 4096
D = 1024
H = 16
FF = 4096
NBLK = 8
BLK = 512
NQB = 8
QB = 512
NPAIR = 8
NKC = 32
EPS = 1e-5
EXP_SCALE = 2.0 ** -13
EXP_BIAS = -30.0
C_ATTN = (S / math.sqrt(S - 1.0)) * (2.0 ** -16)


class _Sem:
    __slots__ = ("name", "n", "h")

    def __init__(self, name):
        self.name = name
        self.n = 0
        self.h = None


DEBUG_TAPS = False
_B_SERIAL = False


def build_program():
    nc = bass.Bass("TRN2", target_bir_lowering=False)
    _scr = "ExternalOutput" if DEBUG_TAPS else "Internal"

    x0_d = nc.dram_tensor("x0", [S, D], F32, kind="ExternalInput").ap()
    wqkvT_d = nc.dram_tensor("wqkvT", [D, 3 * D], F32, kind="ExternalInput").ap()
    woq_d = nc.dram_tensor("woq", [64, 16384], F32, kind="ExternalInput").ap()
    wfc_d = nc.dram_tensor("wfc", [FF, D], F32, kind="ExternalInput").ap()
    wproj_d = nc.dram_tensor("wproj", [D, FF], F32, kind="ExternalInput").ap()
    vT_d = nc.dram_tensor("vT", [D, 16], F32, kind="ExternalInput").ap()
    fbT_d = nc.dram_tensor("fbT", [16, 16], F32, kind="ExternalInput").ap()
    uT_d = nc.dram_tensor("uT", [16, FF], F32, kind="ExternalInput").ap()
    gate_d = nc.dram_tensor("gate", [1, 1], F32, kind="ExternalInput").ap()
    out_d = nc.dram_tensor("out", [S, D], F32, kind="ExternalOutput").ap()

    xn1_scr = nc.dram_tensor("xn1_scr", [S, D], BF16, kind="Internal").ap()
    q_scr = nc.dram_tensor("q_scr", [D, S], BF16, kind=_scr).ap()
    x1_scr = nc.dram_tensor("x1_scr", [S, D], F32, kind=_scr).ap()
    xn2_scr = nc.dram_tensor("xn2_scr", [S, D], BF16, kind=_scr).ap()
    woq_bf_d = nc.dram_tensor("woq_bf", [64, 16384], BF16, kind="Internal").ap()
    wfct_scr = nc.dram_tensor("wfct_scr", [FF, D], BF16, kind="Internal").ap()
    wprjt_scr = nc.dram_tensor("wprjt_scr", [D, FF], BF16, kind=_scr).ap()
    wfcT_scr = nc.dram_tensor("wfcT_scr", [D, FF], BF16, kind=_scr).ap()
    wprjT_scr = nc.dram_tensor("wprjT_scr", [FF, D], BF16, kind=_scr).ap()
    rec_d = nc.dram_tensor("rec_d", [1, 1024], F32, kind="Internal").ap()
    q_scr_kind = _scr
    attn_dbg = nc.dram_tensor("attn_dbg", [64, 8192], BF16, kind=_scr).ap() if DEBUG_TAPS else None
    proj_dbg = nc.dram_tensor("proj_dbg", [512, 1024], F32, kind=_scr).ap() if DEBUG_TAPS else None
    woq_dbg = nc.dram_tensor("woq_dbg", [64, 8192], BF16, kind=_scr).ap() if DEBUG_TAPS else None
    kT_dbg = nc.dram_tensor("kT_dbg", [128, 32768], BF16, kind=_scr).ap() if DEBUG_TAPS else None
    va_dbg = nc.dram_tensor("va_dbg", [128, 33280], BF16, kind=_scr).ap() if DEBUG_TAPS else None

    events = []
    sems = []

    def sem(name):
        s = _Sem(name)
        sems.append(s)
        return s

    sMM = sem("sMM")
    sEX = sem("sEX")
    sV = sem("sV")
    sIO = sem("sIO")
    sN1 = sem("sN1")
    sX1 = sem("sX1")
    sXN2 = sem("sXN2")
    sQW = sem("sQW")
    sQR = sem("sQR")
    sWL = sem("sWL")
    sTL = sem("sTL")
    sTW = sem("sTW")
    sTT = sem("sTT")
    sT1 = sem("sT1")
    sWB = sem("sWB")
    sOUT = sem("sOUT")
    sT1A = sem("sT1A")
    sT1B = sem("sT1B")
    sXTA = sem("sXTA")
    sXTB = sem("sXTB")
    sQRA = sem("sQRA")
    sQRB = sem("sQRB")
    sBC = sem("sBC")
    sWBA = sem("sWBA")
    sWBB = sem("sWBB")
    sWF = sem("sWF")
    sTLA = sem("sTLA")
    sTLB = sem("sTLB")
    sXLA = sem("sXLA")
    sXLB = sem("sXLB")

    def ev(eng, waits, fn):
        events.append((eng, [(s, v) for (s, v) in waits if v > 0], fn))

    def op(eng, emit1, s, waits=(), k=1):
        s.n += k
        tgt = s.n

        def f(e, emit1=emit1, s=s, k=k):
            emit1(e).then_inc(s.h, k)
        ev(eng, waits, f)
        return tgt

    def dma(q, out, in_, s, waits=(), transpose=False, chain=True):
        prev = s.n
        s.n += 16
        tgt = s.n
        w = list(waits)
        if chain and prev > 0:
            w.append((s, prev))

        def f(e, out=out, in_=in_, s=s, transpose=transpose):
            e.dma_start(out=out, in_=in_, transpose=transpose).then_inc(s.h, 16)
        ev(q, w, f)
        return tgt

    with ExitStack() as stack:
        ent = stack.enter_context
        pool_a = ent(nc.sbuf_tensor("pool_a", [128, 33280], BF16))
        pool_k = ent(nc.sbuf_tensor("pool_k", [128, 32768], BF16))
        pool_w = ent(nc.sbuf_tensor("pool_w", [128, 16384], BF16))
        pool_x32 = ent(nc.sbuf_tensor("pool_x32", [128, 4096], F32))
        pool_x16 = ent(nc.sbuf_tensor("pool_x16", [128, 10240], BF16))
        pool_st = ent(nc.sbuf_tensor("pool_st", [128, 2048], BF16))
        pool_q = ent(nc.sbuf_tensor("pool_q", [128, 1024], BF16))
        pool_m = ent(nc.sbuf_tensor("pool_m", [128, 192], F32))

        for s in sems:
            s.h = ent(nc.semaphore(s.name))

        PS_A = ent(nc.psum_tensor("PS_A", [128, 1024], F32))
        PS_B = ent(nc.psum_tensor("PS_B", [128, 1024], F32))
        PS_C = ent(nc.psum_tensor("PS_C", [128, 1024], F32))
        PS_D = ent(nc.psum_tensor("PS_D", [128, 1024], F32))

        c_exp_bias = pool_m[:, 0:1]
        c_exp_scale = pool_m[:, 1:2]
        c_eps = pool_m[:, 2:3]
        c_mhalf = pool_m[:, 3:4]
        c_mone = pool_m[:, 4:5]
        g16 = pool_m[0:16, 8:9]
        bn6 = pool_m[:, 9:21]
        bn6b = pool_m[:, 21:33]
        stats = pool_m[:, 33:41]
        stats2 = pool_m[:, 41:49]
        lnv = pool_m[:, 49:53]
        lnv2 = pool_m[:, 53:57]
        rstd = pool_m[:, 57:61]
        rstd2 = pool_m[:, 61:65]
        s_col = pool_m[:, 65:66]
        h_col = pool_m[:, 66:67]
        nh_col = pool_m[:, 67:68]
        tpart = pool_m[:, 68:72]
        ones16 = pool_m[0:1, 72:88]
        cones = pool_m[64:65, 88:152]
        gate_sb = pool_m[0:1, 152:153]

        def mset(ap, val):
            def f(e, ap=ap, val=val):
                return e.memset(ap, val)
            return f

        op("vector", mset(c_exp_bias, EXP_BIAS), sV)
        op("vector", mset(c_exp_scale, EXP_SCALE), sV)
        op("vector", mset(c_eps, EPS), sV)
        op("vector", mset(c_mhalf, -0.5), sV)
        op("vector", mset(c_mone, -1.0), sV)
        op("vector", mset(ones16, 1.0), sV)
        op("vector", mset(cones, C_ATTN), sV)
        va_ones = pool_a[:].rearrange("p (c h w) -> p c h w", h=16, w=65)[:, :, :, 64:65]
        cst_done = op("vector", mset(va_ones, 1.0), sV)

        g_load = dma("gpsimd", gate_sb, gate_d, sWL)
        g_mm = op("tensor", lambda e: e.matmul(PS_C[0:16, 0:1], ones16, gate_sb,
                                               start=True, stop=True), sMM,
                  waits=[(sWL, g_load), (sV, cst_done)])
        g_cp = op("vector", lambda e: e.tensor_copy(g16, PS_C[0:16, 0:1]), sV, waits=[(sMM, g_mm)])

        wq_sb = pool_w[:, 0:8192]
        wq_t = dma("gpsimd", wq_sb[:].rearrange("p (c f) -> p c f", f=D),
                   wqkvT_d[:, 0:D].rearrange("(c p) f -> p c f", p=128), sWL)
        woq_stage = pool_k[0:64, 0:16384]
        woq_t = dma("gpsimd", woq_stage, woq_d, sWL)
        woq_w = dma("gpsimd", woq_bf_d, woq_stage, sQW, waits=[(sWL, woq_t)])

        x0blk = pool_x32[:, 0:4096]
        xn1blks = [pool_x16[:, 0:4096], pool_x16[:, 4096:8192]]
        xn1T = [pool_w[:, 8192 + i * 4096: 8192 + (i + 1) * 4096] for i in range(2)]
        qstage = [pool_q[:, i * 512:(i + 1) * 512] for i in range(2)]

        ln_done = [0] * NBLK
        n1_done = [0] * NBLK
        q_mm_done = [0] * NBLK
        q_evac = [0] * (NBLK * 8)
        qw_tgts = [0] * (NBLK * 8)

        n1_prev = [0]
        x0h = [pool_x32[:, 0:2048], pool_x32[:, 2048:4096]]
        half_norm_done = [0] * (2 * NBLK)
        blk_tt = [None] * NBLK

        def emit_ln_half(b, hh):
            xn1blk = xn1blks[b % 2]
            gh = 2 * b + hh
            xw = [(sV, half_norm_done[gh - 2])] if gh >= 2 else [(sV, cst_done)]
            x_t = dma("sync", x0h[gh % 2][:].rearrange("p (t d) -> p t d", d=D),
                      x0_d[b * BLK + hh * 256: b * BLK + (hh + 1) * 256, :]
                      .rearrange("(t p) d -> p t d", p=128),
                      sIO, waits=xw)
            ag_t = 0
            for t2 in range(2):
                t = hh * 2 + t2
                buf = bn6 if t % 2 == 0 else bn6b
                srcx = x0h[gh % 2][:, t2 * 1024:(t2 + 1) * 1024]

                def f_st(e, buf=buf, srcx=srcx):
                    e.bn_stats(buf[:, 0:6], srcx[:, 0:512])
                    return e.bn_stats(buf[:, 6:12], srcx[:, 512:1024])
                st_t = op("vector", f_st, sV, waits=[(sIO, x_t)])
                ag_t = op("vector",
                          (lambda e, buf=buf, t=t: e.bn_aggr(stats[:, 2 * t:2 * t + 2],
                                                             buf[:, 0:12])),
                          sV, waits=[(sV, st_t)])
            ex_t = 0
            for t2 in range(2):
                t = hh * 2 + t2
                l_t = op("scalar", (lambda e, t=t: e.activation(lnv[:, t:t + 1],
                                                                stats[:, 2 * t + 1:2 * t + 2],
                                                                AF.Ln, bias=c_eps)),
                         sEX, waits=[(sV, ag_t)] if t2 == 0 else [])
                ex_t = op("scalar", (lambda e, t=t: e.activation(rstd[:, t:t + 1], lnv[:, t:t + 1],
                                                                 AF.Exp, scale=c_mhalf)),
                          sEX, waits=[(sEX, l_t)])
            nt = 0
            for t2 in range(2):
                t = hh * 2 + t2

                def f_nm(e, t=t, t2=t2, xn1blk=xn1blk, gh=gh):
                    return e.tensor_scalar(xn1blk[:, t * 1024:(t + 1) * 1024],
                                           x0h[gh % 2][:, t2 * 1024:(t2 + 1) * 1024],
                                           stats[:, 2 * t:2 * t + 1], rstd[:, t:t + 1],
                                           AluOpType.subtract, AluOpType.mult)
                nmw = [(sEX, ex_t)] if t2 == 0 else []
                if t2 == 0 and b >= 2 and hh == 0:
                    nmw.append((sN1, n1_done[b - 2]))
                nt = op("vector", f_nm, sV, waits=nmw)
            half_norm_done[gh] = nt
            n1_t = dma("sync",
                       xn1_scr[b * BLK + hh * 256: b * BLK + (hh + 1) * 256, :]
                       .rearrange("(t p) d -> p t d", p=128),
                       xn1blk[:, hh * 2048:(hh + 1) * 2048].rearrange("p (t d) -> p t d", d=D),
                       sN1, waits=[(sV, nt)])
            ring = xn1T[b % 2]
            rsem = [sT1A, sT1B][b % 2]
            trw = [(sN1, n1_t)]
            if hh == 0 and b >= 2:
                trw.append((sMM, q_mm_done[b - 2]))
            tt = 0
            for c in range(8):
                tt = dma("scalar", ring[:, c * 512 + hh * 256: c * 512 + hh * 256 + 256],
                         xn1_scr[b * BLK + hh * 256: b * BLK + (hh + 1) * 256,
                                 c * 128:(c + 1) * 128],
                         rsem, waits=trw if c == 0 else [], transpose=True, chain=False)
            if hh == 1:
                ln_done[b] = nt
                n1_done[b] = n1_t
                n1_prev[0] = n1_t
                blk_tt[b] = (rsem, tt)

        def emit_q_tiles(b, f0, f1):
            ring = xn1T[b % 2]
            rsem, tt = blk_tt[b]
            for f in range(f0, f1):
                idx = b * 8 + f
                bank = [PS_A[:, 0:512], PS_A[:, 512:1024]][idx % 2]

                def f_q(e, ring=ring, f=f, bank=bank):
                    r = None
                    for c in range(8):
                        r = e.matmul(bank, wq_sb[:, c * 1024 + f * 128: c * 1024 + (f + 1) * 128],
                                     ring[:, c * 512:(c + 1) * 512], start=(c == 0), stop=(c == 7))
                    return r
                qw = [(rsem, tt), (sWL, wq_t)]
                if idx >= 2:
                    qw.append((sV, q_evac[idx - 2]))
                qm = op("tensor", f_q, sMM, waits=qw)
                q_mm_done[b] = qm
                qs = qstage[idx % 2]
                qev_w = [(sMM, qm)]
                if idx >= 2:
                    qev_w.append((sQW, qw_tgts[idx - 2]))
                qev = op("vector", (lambda e, bank=bank, qs=qs: e.tensor_copy(qs, bank)),
                         sV, waits=qev_w)
                q_evac[idx] = qev
                qw_tgts[idx] = dma("gpsimd", q_scr[f * 128:(f + 1) * 128, b * BLK:(b + 1) * BLK],
                                   qs, sQW, waits=[(sV, qev)])

        for b in range(NBLK):
            emit_ln_half(b, 0)
            if b >= 1:
                emit_q_tiles(b - 1, 0, 4)
            emit_ln_half(b, 1)
            if b >= 1:
                emit_q_tiles(b - 1, 4, 8)
        emit_q_tiles(NBLK - 1, 0, 8)

        pass1_mm = sMM.n
        pass1_v = sV.n
        last_qw = sQW.n

        wk_sb = pool_w[:, 0:8192]
        wv_sb = pool_w[:, 8192:16384]
        wk_t = dma("gpsimd", wk_sb[:].rearrange("p (c f) -> p c f", f=D),
                   wqkvT_d[:, D:2 * D].rearrange("(c p) f -> p c f", p=128),
                   sWL, waits=[(sMM, pass1_mm), (sV, pass1_v)])
        wv_t = dma("gpsimd", wv_sb[:].rearrange("p (c f) -> p c f", f=D),
                   wqkvT_d[:, 2 * D:3 * D].rearrange("(c p) f -> p c f", p=128),
                   sWL, waits=[(sMM, pass1_mm)])

        xn1T2 = [pool_x16[:, i * 4096:(i + 1) * 4096] for i in range(2)]
        kT = pool_k
        v_aug4 = pool_a[:].rearrange("p (c h w) -> p c h w", h=16, w=65)

        k_evac = [0] * (NBLK * 8)
        v_evac = [0] * (NBLK * 8)
        blk_mm_end = [0] * NBLK

        for b in range(NBLK):
            ring = xn1T2[b % 2]
            rsem = [sT1A, sT1B][b % 2]
            trw = [(sN1, 16 * NBLK)]
            if b == 0:
                trw += [(sWL, wv_t), (sV, pass1_v)]
            if b >= 2:
                trw.append((sMM, blk_mm_end[b - 2]))
            tt = 0
            for c in range(8):
                tt = dma("scalar", ring[:, c * 512:(c + 1) * 512],
                         xn1_scr[b * BLK:(b + 1) * BLK, c * 128:(c + 1) * 128],
                         rsem, waits=trw if c == 0 else [], transpose=True, chain=False)
            for f in range(8):
                idx = b * 8 + f
                bank = [PS_A[:, 0:512], PS_A[:, 512:1024]][idx % 2]

                def f_k(e, ring=ring, f=f, bank=bank):
                    r = None
                    for c in range(8):
                        r = e.matmul(bank, wk_sb[:, c * 1024 + f * 128: c * 1024 + (f + 1) * 128],
                                     ring[:, c * 512:(c + 1) * 512], start=(c == 0), stop=(c == 7))
                    return r
                kw = [(rsem, tt), (sWL, wk_t)]
                if idx >= 2:
                    kw.append((sV, k_evac[idx - 2]))
                km = op("tensor", f_k, sMM, waits=kw)
                k_evac[idx] = op("vector",
                                 (lambda e, bank=bank, b=b, f=f:
                                  e.tensor_copy(kT[:, f * 4096 + b * 512: f * 4096 + (b + 1) * 512], bank)),
                                 sV, waits=[(sMM, km)])
                blk_mm_end[b] = km
            for j in range(4):
                for hf in range(2):
                    idx = b * 8 + j * 2 + hf
                    bank = [PS_B[:, 0:512], PS_B[:, 512:1024]][idx % 2]

                    def f_v(e, ring=ring, j=j, hf=hf, bank=bank):
                        r = None
                        for c in range(8):
                            r = e.matmul(bank, ring[:, c * 512 + j * 128: c * 512 + (j + 1) * 128],
                                         wv_sb[:, c * 1024 + hf * 512: c * 1024 + (hf + 1) * 512],
                                         start=(c == 0), stop=(c == 7))
                        return r
                    vw = [(rsem, tt), (sWL, wv_t)]
                    if idx >= 2:
                        vw.append((sV, v_evac[idx - 2]))
                    vm = op("tensor", f_v, sMM, waits=vw)
                    blk_mm_end[b] = vm
                    cg = b * 4 + j
                    dst = v_aug4[:, cg, hf * 8:(hf + 1) * 8, 0:64]
                    src = bank.rearrange("p (h w) -> p h w", w=64)
                    v_evac[idx] = op("vector", (lambda e, dst=dst, src=src: e.tensor_copy(dst, src)),
                                     sV, waits=[(sMM, vm)])

        a1_mm = sMM.n
        a1_v = sV.n

        st_ring = [pool_st[:, i * 1024:(i + 1) * 1024] for i in range(2)]
        qring = [pool_q[:, i * 512:(i + 1) * 512] for i in range(2)]
        attn_sb = pool_w[0:64, 0:8192]
        woqq = [pool_w[0:64, 8192:12288], pool_w[0:64, 12288:16384]]
        woqr = pool_w[0:64, 8192:16384]
        x0t = pool_x32[:, 0:1024]
        aout = pool_x32[:, 1024:2048]
        x1t = pool_x32[:, 2048:3072]
        bcrec = pool_x32[:, 3072:4096]
        rec = bcrec[64:65, :]
        bc_sb = bcrec[0:64, :]
        xn2ring = [pool_x16[:, i * 1024:(i + 1) * 1024] for i in range(2)]

        pxf = pool_x16.bitcast(F32)
        wfst = [pxf[:, 1024 + i * 512: 1024 + (i + 1) * 512 + 512] for i in range(0)] or \
               [pxf[:, 1024:2048], pxf[:, 2048:3072]]
        stg16 = [pool_x16[:, 2048 + i * 2048: 2048 + (i + 1) * 2048] for i in range(2)]
        tern_a = pool_x16[:, 6144:7168]
        tern_b = pool_x16[:, 7168:8192]
        t1ring = [pool_x16[:, 8192 + i * 1024: 8192 + (i + 1) * 1024] for i in range(2)]

        slot_w_guard = [0, 0]
        slot_v_guard = [0, 0]
        last_comb = {"n": 0}
        tern_w_done = {"n": 0}

        def tern_compare_chain(j, dst_dram):
            g1 = op("vector", (lambda e, j=j: e.tensor_single_scalar(tern_a, wfst[j], h_col[:, 0:1],
                                                                     AluOpType.is_ge)),
                    sV, waits=[(sV, last_comb["n"])])
            g2 = op("vector", (lambda e, j=j: e.tensor_single_scalar(tern_b, wfst[j], nh_col[:, 0:1],
                                                                     AluOpType.is_le)),
                    sV, waits=[(sV, g1)])
            c1 = op("vector", (lambda e, j=j: e.tensor_sub(stg16[j][:, 0:1024], tern_a, tern_b)),
                    sV, waits=[(sV, g2)])
            c2 = op("vector", (lambda e, j=j: e.tensor_scalar_mul(stg16[j][:, 1024:2048],
                                                                  stg16[j][:, 0:1024], s_col[:, 0:1])),
                    sV, waits=[(sV, c1)])
            wt = dma("gpsimd", dst_dram, stg16[j][:, 1024:2048], sTW, waits=[(sV, c2)])
            last_comb["n"] = c2
            slot_w_guard[j] = wt
            slot_v_guard[j] = c2
            return wt

        def stage_load(j, src):
            w = []
            if slot_v_guard[j]:
                w.append((sV, slot_v_guard[j]))
            if slot_w_guard[j]:
                w.append((sTW, slot_w_guard[j]))
            return dma("gpsimd", wfst[j], src, [sTLA, sTLB][j], waits=w, chain=False)

        tern_seq = {"i": 0}

        def emit_tern_fc(i):
            j = i % 2
            lt = stage_load(j, wfc_d[i * 128:(i + 1) * 128, :])
            r1 = op("vector", (lambda e, j=j: e.reduce_sum(s_col, wfst[j], AX.X,
                                                           apply_absolute_value=True)),
                    sV, waits=[([sTLA, sTLB][j], lt)])
            r2 = op("vector", lambda e: e.tensor_scalar(s_col, s_col, 1.0 / D, 1e-5,
                                                        AluOpType.mult, AluOpType.max),
                    sV, waits=[(sV, r1)])
            r3 = op("vector", lambda e: e.tensor_scalar_mul(h_col, s_col, 0.5), sV, waits=[(sV, r2)])
            r4 = op("vector", lambda e: e.tensor_scalar_mul(nh_col, s_col, -0.5), sV, waits=[(sV, r3)])
            last_comb["n"] = max(last_comb["n"], r4)
            slot_v_guard[j] = max(slot_v_guard[j], r4)
            tern_w_done["n"] = tern_compare_chain(j, wfct_scr[i * 128:(i + 1) * 128, :])

        def emit_tern_proj(i):
            ch_last = 0
            for c in range(4):
                j = tern_seq["i"] % 2
                tern_seq["i"] += 1
                lt = stage_load(j, wproj_d[i * 128:(i + 1) * 128, c * 1024:(c + 1) * 1024])
                ch_last = op("vector", (lambda e, j=j, c=c: e.reduce_sum(tpart[:, c:c + 1], wfst[j], AX.X,
                                                                        apply_absolute_value=True)),
                             sV, waits=[([sTLA, sTLB][j], lt)])
                slot_v_guard[j] = ch_last
            r1 = op("vector", lambda e: e.reduce_sum(s_col, tpart, AX.X), sV, waits=[(sV, ch_last)])
            r2 = op("vector", lambda e: e.tensor_scalar(s_col, s_col, 1.0 / FF, 1e-5,
                                                        AluOpType.mult, AluOpType.max),
                    sV, waits=[(sV, r1)])
            r3 = op("vector", lambda e: e.tensor_scalar_mul(h_col, s_col, 0.5), sV, waits=[(sV, r2)])
            r4 = op("vector", lambda e: e.tensor_scalar_mul(nh_col, s_col, -0.5), sV, waits=[(sV, r3)])
            last_comb["n"] = max(last_comb["n"], r4)
            for c in range(4):
                j = tern_seq["i"] % 2
                tern_seq["i"] += 1
                lt = stage_load(j, wproj_d[i * 128:(i + 1) * 128, c * 1024:(c + 1) * 1024])
                g_pre = op("vector", (lambda e: e.engine_nop()), sV, waits=[([sTLA, sTLB][j], lt)])
                last_comb["n"] = max(last_comb["n"], g_pre)
                tern_w_done["n"] = tern_compare_chain(
                    j, wprjt_scr[i * 128:(i + 1) * 128, c * 1024:(c + 1) * 1024])

        tern_units = [("fc", i) for i in range(32)] + [("proj", i) for i in range(8)]
        tern_iter = iter(tern_units)

        t1_units = [("fc", a, fb) for a in range(8) for fb in range(8)] + \
                   [("proj", a, 0) for a in range(32)]
        t1_iter = iter(t1_units)
        t1_state = {"cnt": 0, "busy": [0, 0]}

        def emit_t1_unit(u):
            kind, a, bnum = u
            j = t1_state["cnt"] % 2
            t1_state["cnt"] += 1
            w = [(sTW, tern_w_done["n"])]
            if t1_state["busy"][j]:
                w.append((sTT, t1_state["busy"][j]))
            tsem = [sT1A, sT1B][j]
            if kind == "fc":
                rt = dma("sync", t1ring[j][:, 0:512],
                         wfct_scr[bnum * 512:(bnum + 1) * 512, a * 128:(a + 1) * 128],
                         tsem, waits=w, transpose=True, chain=False)
                wt = dma("gpsimd", wfcT_scr[a * 128:(a + 1) * 128, bnum * 512:(bnum + 1) * 512],
                         t1ring[j][:, 0:512], sTT, waits=[(tsem, rt)])
            else:
                rt = dma("sync", t1ring[j],
                         wprjt_scr[0:D, a * 128:(a + 1) * 128],
                         tsem, waits=w, transpose=True, chain=False)
                wt = dma("gpsimd", wprjT_scr[a * 128:(a + 1) * 128, :], t1ring[j],
                         sTT, waits=[(tsem, rt)])
            t1_state["busy"][j] = wt

        def emit_side_work(group_idx):
            if group_idx < 40:
                u = next(tern_iter, None)
                if u is not None:
                    if u[0] == "fc":
                        emit_tern_fc(u[1])
                    else:
                        emit_tern_proj(u[1])
            else:
                for _ in range(14):
                    u = next(t1_iter, None)
                    if u is not None:
                        emit_t1_unit(u)

        if DEBUG_TAPS:
            dma("sync", kT_dbg, kT[:, 0:32768], sIO, waits=[(sV, a1_v), (sMM, a1_mm)])
            dma("sync", va_dbg, pool_a[:, 0:33280], sIO, waits=[(sV, a1_v), (sMM, a1_mm)])

        x1t65 = pool_x32[0:65, 2048:3072]
        x1t64 = pool_x32[0:64, 2048:3072]
        REGS = [PS_A, PS_B, PS_D]
        q_tgt = [0] * NPAIR
        pair_evac_done = a1_v
        ps_c_free = a1_v
        prev_exp_last3 = [0, 0, 0]
        prev_exp_psD = 0
        last_proj_mm = a1_mm
        psD_free_v = a1_v
        last_scores_mm = 0
        last_attnv_mm = 0
        prev_ln2_sc = 0
        ln2_prev_qb = 0
        prev_add1 = 0
        x1_written = [0] * NQB
        xn2_written = [0] * NQB
        xn2w_slot = [0, 0]
        woq_slot_free = a1_mm
        woq_slot_free_b = a1_mm
        woqr_pre = [0, 0]
        reg_ln_free = [0, 0, 0, 0]
        group = 0
        side_cnt = [0]
        prev_rcw = [0]
        prev_bcd = [0]
        prev_x1w = [0]

        qrd_pending = [0, 0]
        for qb in range(NQB):
            for p in range(NPAIR):
                if qb == 0 and p == 0:
                    qrd_pending[0] = dma("gpsimd", qring[0], q_scr[0:128, 0:QB], sQRA,
                                         waits=[(sQW, last_qw), (sV, a1_v)], chain=False)
                    qrd_pending[1] = dma("gpsimd", qring[1], q_scr[128:256, 0:QB], sQRB,
                                         waits=[(sQW, last_qw)], chain=False)
                qsem = [sQRA, sQRB][p % 2]
                qrd = qrd_pending[p % 2]

                exp_tgt = [0] * NKC
                attnv_tgt = [0] * NKC
                sc_tgt = [0] * NKC

                def emit_scores(kc, p=p, qb=qb, qrd=qrd, sc_tgt=sc_tgt, exp_tgt=exp_tgt):
                    REG = REGS[kc % 3]
                    w = []
                    if kc >= 3:
                        w.append((sEX, exp_tgt[kc - 3]))
                    else:
                        w.append((sEX, prev_exp_last3[kc]))
                        if p == 0 and reg_ln_free[kc]:
                            w.append((sV, reg_ln_free[kc if kc < 2 else 3]))
                    if kc == 0:
                        w.append((qsem, qrd))
                        if qb == 0 and p == 0:
                            w.append((sV, a1_v))
                            w.append((sMM, a1_mm))

                    def f(e, REG=REG, p=p, kc=kc):
                        e.matmul(REG[:, 0:512],
                                 kT[0:64, p * 4096 + kc * 128: p * 4096 + (kc + 1) * 128],
                                 qring[p % 2][0:64, :], start=True, stop=True, tile_position=(0, 0))
                        return e.matmul(REG[:, 512:1024],
                                        kT[64:128, p * 4096 + kc * 128: p * 4096 + (kc + 1) * 128],
                                        qring[p % 2][64:128, :], start=True, stop=True,
                                        tile_position=(64, 0))
                    sc_tgt[kc] = op("tensor", f, sMM, waits=w)

                def emit_exp(kc, sc_tgt=sc_tgt, attnv_tgt=attnv_tgt, exp_tgt=exp_tgt):
                    REG = REGS[kc % 3]
                    w = [(sMM, sc_tgt[kc])]
                    if kc >= 2:
                        w.append((sMM, attnv_tgt[kc - 2]))

                    def f(e, REG=REG, kc=kc):
                        return e.activation(st_ring[kc % 2], REG[:, 0:1024], AF.Exp,
                                            bias=c_exp_bias, scale=c_exp_scale)
                    exp_tgt[kc] = op("scalar", f, sEX, waits=w)

                def emit_attnv(kc, p=p, exp_tgt=exp_tgt, attnv_tgt=attnv_tgt):
                    w = [(sEX, exp_tgt[kc])]
                    if kc == 0:
                        w.append((sV, ps_c_free))

                    def f(e, p=p, kc=kc):
                        e.matmul(PS_C[0:65, 0:512], v_aug4[:, kc, 2 * p, :],
                                 st_ring[kc % 2][:, 0:512], start=(kc == 0), stop=(kc == NKC - 1))
                        return e.matmul(PS_C[0:65, 512:1024], v_aug4[:, kc, 2 * p + 1, :],
                                        st_ring[kc % 2][:, 512:1024],
                                        start=(kc == 0), stop=(kc == NKC - 1))
                    attnv_tgt[kc] = op("tensor", f, sMM, waits=w)

                for kc in range(NKC):
                    emit_scores(kc)
                    if kc >= 1:
                        emit_attnv(kc - 1)
                    emit_exp(kc)
                emit_attnv(NKC - 1)
                q_tgt[p] = sc_tgt[NKC - 1]
                last_scores_mm = sc_tgt[NKC - 1]
                last_attnv_mm = attnv_tgt[NKC - 1]
                prev_exp_last3 = [exp_tgt[NKC - 2], exp_tgt[NKC - 1], exp_tgt[NKC - 3]]

                cw = [(sMM, attnv_tgt[NKC - 1])]
                if p == 0:
                    cw.append((sV, prev_add1))
                vcp_att = op("vector", lambda e: e.tensor_copy(x1t65, PS_C[0:65, 0:1024]),
                             sV, waits=cw)
                ps_c_free = vcp_att
                if p == NPAIR - 1:
                    lw = [(sV, vcp_att)]
                    if prev_rcw[0]:
                        lw.append((sBC, prev_rcw[0]))
                    lnr = op("scalar", lambda e: e.activation(aout[64:65, 0:1024],
                                                              x1t65[64:65, 0:1024], AF.Ln),
                             sEX, waits=lw)
                    vr = op("scalar", lambda e: e.activation(rec, aout[64:65, 0:1024],
                                                             AF.Exp, scale=c_mone[64:65, 0:1]),
                            sEX, waits=[(sEX, lnr)])
                    rcw_w = [(sEX, vr)]
                else:
                    vr_w = [(sV, vcp_att)]
                    if prev_rcw[0]:
                        vr_w.append((sBC, prev_rcw[0]))
                    vr = op("vector", lambda e: e.reciprocal(rec, x1t65[64:65, 0:1024]),
                            sV, waits=vr_w)
                    rcw_w = [(sV, vr)]
                if prev_bcd[0]:
                    rcw_w.append((sBC, prev_bcd[0]))
                rcw = dma("gpsimd", rec_d, rec, sBC, waits=rcw_w)
                prev_rcw[0] = rcw
                bcd = dma("gpsimd", bc_sb, rec_d.broadcast_to([64, 1024]), sBC,
                          waits=[(sBC, rcw)])
                prev_bcd[0] = bcd
                va1_w = [(sBC, bcd)]
                if p == 0:
                    va1_w.append((sMM, last_proj_mm))
                va1 = op("vector",
                         (lambda e, p=p: e.scalar_tensor_tensor(
                             attn_sb[:, (2 * p) * 512:(2 * p + 1) * 512],
                             x1t64[:, 0:512], C_ATTN, bc_sb[:, 0:512],
                             AluOpType.mult, AluOpType.mult)),
                         sV, waits=va1_w)
                va2 = op("vector",
                         (lambda e, p=p: e.scalar_tensor_tensor(
                             attn_sb[:, (2 * p + 1) * 512:(2 * p + 2) * 512],
                             x1t64[:, 512:1024], C_ATTN, bc_sb[:, 512:1024],
                             AluOpType.mult, AluOpType.mult)),
                         sV, waits=[(sV, va1)])
                pair_evac_done = va2
                prev_exp_psD = exp_tgt[29]

                g2 = qb * NPAIR + p + 2
                if g2 < NQB * NPAIR:
                    qb2, p2 = divmod(g2, NPAIR)
                    qrd_pending[g2 % 2] = dma(
                        "gpsimd", qring[g2 % 2],
                        q_scr[p2 * 128:(p2 + 1) * 128, qb2 * QB:(qb2 + 1) * QB],
                        [sQRA, sQRB][g2 % 2], waits=[(sMM, q_tgt[p])], chain=False)

                if group % 8 <= 5:
                    emit_side_work(side_cnt[0])
                    side_cnt[0] += 1
                if p == 6:
                    woqr_pre[0] = dma("sync", woqr, woq_bf_d[:, 0:8192], sWBA,
                                      waits=[(sMM, woq_slot_free)] +
                                      ([(sQW, woq_w)] if qb == 0 else []), chain=False)
                    woqr_pre[1] = sWBB.n
                group += 1

            if DEBUG_TAPS and qb == NQB - 1:
                dma("sync", attn_dbg, attn_sb, sIO, waits=[(sV, pair_evac_done)])
            qtB2 = [PS_A, PS_B, PS_C, PS_D]
            bank_guards = [[(sEX, prev_exp_last3[0])], [(sEX, prev_exp_last3[1])],
                           [(sV, ps_c_free)], [(sEX, prev_exp_psD)]]
            proj_done = [0] * 4
            for nh in range(2):
                if nh == 0:
                    wl = woqr_pre[0]
                else:
                    wl = dma("sync", woqr, woq_bf_d[:, 8192:16384], sWBB,
                             waits=[(sMM, proj_done[3]), (sWBB, woqr_pre[1])], chain=False)
                for qt in range(4):
                    def f_pr(e, qt=qt, nh=nh):
                        r = None
                        for h in range(H):
                            r = e.matmul(qtB2[qt][:, nh * 512:(nh + 1) * 512],
                                         attn_sb[:, h * 512 + qt * 128: h * 512 + (qt + 1) * 128],
                                         woqr[:, h * 512:(h + 1) * 512],
                                         start=(h == 0), stop=(h == H - 1))
                        return r
                    pw = [([sWBA, sWBB][nh], wl), (sV, pair_evac_done)]
                    if nh == 0:
                        pw += bank_guards[qt]
                    proj_done[qt] = op("tensor", f_pr, sMM, waits=pw)
                    last_proj_mm = proj_done[qt]
            woq_slot_free = proj_done[3]
            woq_slot_free_b = proj_done[3]
            nm1_tgt = [0] * 4
            for qt in range(4):
                xw = [(sV, prev_add1)] if prev_add1 else []
                x_t = dma("sync", x0t, x0_d[qb * QB + qt * 128: qb * QB + (qt + 1) * 128, :],
                          sIO, waits=xw)

                def f_st1(e, qt=qt):
                    e.bn_stats(bn6[:, 0:6], qtB2[qt][:, 0:512])
                    return e.bn_stats(bn6[:, 6:12], qtB2[qt][:, 512:1024])
                st1 = op("vector", f_st1, sV, waits=[(sMM, proj_done[qt]), (sEX, ln2_prev_qb)])
                ag1 = op("vector", (lambda e, qt=qt: e.bn_aggr(stats[:, 2 * qt:2 * qt + 2], bn6[:, 0:12])),
                         sV, waits=[(sV, st1)])
                l1 = op("scalar", (lambda e, qt=qt: e.activation(lnv[:, qt:qt + 1],
                                                                 stats[:, 2 * qt + 1:2 * qt + 2],
                                                                 AF.Ln, bias=c_eps)),
                        sEX, waits=[(sV, ag1)])
                e1 = op("scalar", (lambda e, qt=qt: e.activation(rstd[:, qt:qt + 1], lnv[:, qt:qt + 1],
                                                                 AF.Exp, scale=c_mhalf)),
                        sEX, waits=[(sEX, l1)])
                nm1 = op("vector", (lambda e, qt=qt: e.tensor_scalar(x1t, qtB2[qt][:, 0:1024],
                                                                    stats[:, 2 * qt:2 * qt + 1],
                                                                    rstd[:, qt:qt + 1],
                                                                    AluOpType.subtract, AluOpType.mult)),
                         sV, waits=[(sEX, e1)])
                nm1_tgt[qt] = nm1
                a1w = [(sV, nm1), (sIO, x_t)]
                if prev_x1w[0]:
                    a1w.append((sX1, prev_x1w[0]))
                add1 = op("vector", lambda e: e.tensor_add(aout, x1t, x0t),
                          sV, waits=a1w)
                prev_add1 = add1
                prev_x1w[0] = dma("sync", x1_scr[qb * QB + qt * 128: qb * QB + (qt + 1) * 128, :],
                                  aout, sX1, waits=[(sV, add1)])

                def f_st2(e):
                    e.bn_stats(bn6b[:, 0:6], aout[:, 0:512])
                    return e.bn_stats(bn6b[:, 6:12], aout[:, 512:1024])
                st2 = op("vector", f_st2, sV, waits=[(sV, add1)])
                ag2 = op("vector", (lambda e, qt=qt: e.bn_aggr(stats2[:, 2 * qt:2 * qt + 2], bn6b[:, 0:12])),
                         sV, waits=[(sV, st2)])
                l2 = op("scalar", (lambda e, qt=qt: e.activation(lnv2[:, qt:qt + 1],
                                                                 stats2[:, 2 * qt + 1:2 * qt + 2],
                                                                 AF.Ln, bias=c_eps)),
                        sEX, waits=[(sV, ag2)])
                e2 = op("scalar", (lambda e, qt=qt: e.activation(rstd2[:, qt:qt + 1], lnv2[:, qt:qt + 1],
                                                                 AF.Exp, scale=c_mhalf)),
                        sEX, waits=[(sEX, l2)])
                prev_ln2_sc = e2
                nm2 = op("vector", (lambda e, qt=qt: e.tensor_scalar(xn2ring[qt % 2], aout,
                                                                    stats2[:, 2 * qt:2 * qt + 1],
                                                                    rstd2[:, qt:qt + 1],
                                                                    AluOpType.subtract, AluOpType.mult)),
                         sV, waits=[(sEX, e2), (sXN2, xn2w_slot[qt % 2])])
                xn2w_slot[qt % 2] = dma("sync",
                                        xn2_scr[qb * QB + qt * 128: qb * QB + (qt + 1) * 128, :],
                                        xn2ring[qt % 2], sXN2, waits=[(sV, nm2)])
            x1_written[qb] = sX1.n
            xn2_written[qb] = sXN2.n
            ln2_prev_qb = prev_ln2_sc
            ps_c_free = nm1_tgt[2]
            psD_free_v = nm1_tgt[3]
            reg_ln_free = nm1_tgt

        a2_mm = sMM.n
        a2_v = sV.n
        a2_sc = sEX.n
        a2_bc = sBC.n

        while True:
            u = next(t1_iter, None)
            if u is None:
                break
            emit_t1_unit(u)
        t1_all = sTT.n

        wfcT = pool_k
        wfc_ld = 0
        for hc in range(8):
            w0 = [(sTT, t1_all), (sMM, last_scores_mm)] if hc == 0 else []
            wfc_ld = dma("gpsimd", wfcT[:, hc * 4096:(hc + 1) * 4096],
                         wfcT_scr[hc * 128:(hc + 1) * 128, :], sWF, waits=w0)
        uT_bf = pool_a[0:16, 0:4096]
        vT_bf = pool_a[:, 4096:4224]
        fbT_bf = pool_a[0:16, 4224:4240]
        dma("gpsimd", uT_bf, uT_d, sWB, waits=[(sMM, last_attnv_mm)])
        dma("gpsimd", vT_bf.rearrange("p (c r) -> p c r", r=16),
            vT_d.rearrange("(c p) r -> p c r", p=128), sWB)
        b_w_ready = dma("gpsimd", fbT_bf, fbT_d, sWB)

        xn2T = [pool_x16[:, i * 4096:(i + 1) * 4096] for i in range(2)]
        wprj_ring = [pool_x16[:, 8192 + i * 1024: 8192 + (i + 1) * 1024] for i in range(2)]
        x1r = [pool_x32[:, i * 1024:(i + 1) * 1024] for i in range(2)]
        ln_b = pool_x32[:, 2048:3072]
        x2_b = pool_x32[:, 3072:4096]
        relu_ring = [pool_st[:, i * 512:(i + 1) * 512] for i in range(2)]
        r2_sb = pool_w
        t_sb = pool_q[0:16, 0:512]
        t2_sb = pool_q[0:16, 512:1024]

        fc_tile_done = [0] * 32
        b_fc_end = [0] * NBLK
        b_ln_prev = a2_sc
        fc_bank_free = [a2_v] * 4
        qt_bank_free_v = [a2_v] * 4
        prev_block_mm = a2_mm
        prev_norm = 0
        x2_stored = 0
        x1_slot = [0, 0]
        wprj_free = [a2_mm, a2_mm]
        fcB = [PS_C[:, 0:512], PS_C[:, 512:1024], PS_A[:, 0:512], PS_A[:, 512:1024]]
        qtB = [PS_A, PS_B, PS_C, PS_D]

        b_tr = [None] * NBLK

        def emit_tr(bb, extra):
            xsem = [sXTA, sXTB][bb % 2]
            trw = [(sXN2, xn2_written[bb])] + extra
            tt = 0
            for c in range(8):
                tt = dma("scalar", xn2T[bb % 2][:, c * 512:(c + 1) * 512],
                         xn2_scr[bb * BLK:(bb + 1) * BLK, c * 128:(c + 1) * 128],
                         xsem, waits=trw if c == 0 else [], transpose=True, chain=False)
            b_tr[bb] = (xsem, tt)

        t2ev_of = [0] * NBLK

        def emit_tchain(bb, extra):
            xsem2, tt2 = b_tr[bb]

            def f_t(e, bb=bb):
                r = None
                for c in range(8):
                    r = e.matmul(PS_C[0:16, 0:512], vT_bf[:, c * 16:(c + 1) * 16],
                                 xn2T[bb % 2][:, c * 512:(c + 1) * 512],
                                 start=(c == 0), stop=(c == 7))
                return r
            tw = [(xsem2, tt2), (sWB, b_w_ready), (sV, qt_bank_free_v[2])] + extra
            t_mm = op("tensor", f_t, sMM, waits=tw)
            t_ev = op("vector", lambda e: e.tensor_scalar_mul(t_sb, PS_C[0:16, 0:512], g16[:, 0:1]),
                      sV, waits=[(sMM, t_mm), (sV, g_cp)])
            t2_mm = op("tensor", lambda e: e.matmul(PS_C[0:16, 512:1024], fbT_bf, t_sb,
                                                    start=True, stop=True), sMM, waits=[(sV, t_ev)])
            t2ev_of[bb] = op("vector", lambda e: e.tensor_copy(t2_sb, PS_C[0:16, 512:1024]),
                             sV, waits=[(sMM, t2_mm)])

        emit_tr(0, [(sXN2, xn2_written[NQB - 1]), (sWF, wfc_ld)])
        for b in range(NBLK):
            xsem, tt = b_tr[b]

            if b == 0:
                emit_tchain(0, [(sV, a2_v)])
            t2_ev = t2ev_of[b]

            for f in range(32):
                bank = fcB[f % 4]

                def f_fc(e, b=b, f=f, bank=bank):
                    for c in range(8):
                        e.matmul(bank, wfcT[:, c * 4096 + f * 128: c * 4096 + (f + 1) * 128],
                                 xn2T[b % 2][:, c * 512:(c + 1) * 512], start=(c == 0), stop=False)
                    return e.matmul(bank, uT_bf[:, f * 128:(f + 1) * 128], t2_sb,
                                    start=False, stop=True)
                fw = [(xsem, tt), (sV, t2_ev), (sV, fc_bank_free[f % 4])]
                if b == 0 and f < 4:
                    fw.append((sEX, a2_sc))
                if f < 4:
                    fw.append((sV, qt_bank_free_v[2 if f < 2 else 0]))
                fcm = op("tensor", f_fc, sMM, waits=fw)
                b_fc_end[b] = fcm
                rl_w = [(sMM, fcm)]
                if fc_tile_done[f - 2]:
                    rl_w.append((sV, fc_tile_done[f - 2]))
                rl = op("scalar", (lambda e, bank=bank, f=f: e.activation(relu_ring[f % 2], bank,
                                                                          AF.Relu)),
                        sEX, waits=rl_w)
                sq = op("vector", (lambda e, f=f: e.tensor_mul(r2_sb[:, f * 512:(f + 1) * 512],
                                                               relu_ring[f % 2], relu_ring[f % 2])),
                        sV, waits=[(sEX, rl)])
                fc_tile_done[f] = sq
                fc_bank_free[f % 4] = sq

            if b + 1 < NBLK:
                if b + 1 == 1:
                    emit_tr(1, [(sXN2, xn2_written[NQB - 1]), (sTT, t1_all)])
                else:
                    emit_tr(b + 1, [(sMM, b_fc_end[b - 1])])

            proj_tgt = 0
            for hc in range(32):
                j = hc % 2
                wl_w = [(sMM, wprj_free[j])]
                if b == 0 and hc < 2:
                    wl_w.append((sTT, t1_all))
                bsem = [sWBA, sWBB][j]
                wl = dma("gpsimd", wprj_ring[j], wprjT_scr[hc * 128:(hc + 1) * 128, :],
                         bsem, waits=wl_w, chain=False)

                def f_pj(e, hc=hc, j=j):
                    r = None
                    for qt in range(4):
                        for nh in range(2):
                            r = e.matmul(qtB[qt][:, nh * 512:(nh + 1) * 512],
                                         r2_sb[:, hc * 512 + qt * 128: hc * 512 + (qt + 1) * 128],
                                         wprj_ring[j][:, nh * 512:(nh + 1) * 512],
                                         start=(hc == 0), stop=(hc == 31))
                    return r
                pw = [(bsem, wl), (sV, fc_tile_done[hc])]
                if hc == 0:
                    pw += [(sV, fc_bank_free[0]), (sV, fc_bank_free[1]),
                           (sV, fc_bank_free[2]), (sV, fc_bank_free[3]),
                           (sV, qt_bank_free_v[1]), (sV, qt_bank_free_v[3])]
                proj_tgt = op("tensor", f_pj, sMM, waits=pw)
                wprj_free[j] = proj_tgt
            prev_block_mm = proj_tgt

            for qt in (2, 0, 1, 3):
                x1w = [(sX1, x1_written[b])]
                if x1_slot[qt % 2]:
                    x1w.append((sV, x1_slot[qt % 2]))
                else:
                    x1w += [(sX1, x1_written[NQB - 1]), (sXN2, xn2_written[NQB - 1]),
                            (sBC, a2_bc)]
                lsem = [sXLA, sXLB][qt % 2]
                x1t_b = dma("sync", x1r[qt % 2],
                            x1_scr[b * BLK + qt * 128: b * BLK + (qt + 1) * 128, :],
                            lsem, waits=x1w, chain=False)

                def f_st(e, qt=qt):
                    e.bn_stats(bn6[:, 0:6], qtB[qt][:, 0:512])
                    return e.bn_stats(bn6[:, 6:12], qtB[qt][:, 512:1024])
                st = op("vector", f_st, sV, waits=[(sMM, proj_tgt), (sEX, b_ln_prev)])
                ag = op("vector", (lambda e, qt=qt: e.bn_aggr(stats[:, 2 * qt:2 * qt + 2], bn6[:, 0:12])),
                        sV, waits=[(sV, st)])
                l1 = op("scalar", (lambda e, qt=qt: e.activation(lnv[:, qt:qt + 1],
                                                                 stats[:, 2 * qt + 1:2 * qt + 2],
                                                                 AF.Ln, bias=c_eps)),
                        sEX, waits=[(sV, ag)])
                e1 = op("scalar", (lambda e, qt=qt: e.activation(rstd[:, qt:qt + 1], lnv[:, qt:qt + 1],
                                                                 AF.Exp, scale=c_mhalf)),
                        sEX, waits=[(sEX, l1)])
                prev_ln2_sc = e1
                nm = op("vector", (lambda e, qt=qt: e.tensor_scalar(ln_b, qtB[qt][:, 0:1024],
                                                                   stats[:, 2 * qt:2 * qt + 1],
                                                                   rstd[:, qt:qt + 1],
                                                                   AluOpType.subtract, AluOpType.mult)),
                        sV, waits=[(sEX, e1)])
                qt_bank_free_v[qt] = nm
                ad_w = [(sV, nm), (lsem, x1t_b)]
                if b == 0:
                    ad_w.append((sBC, a2_bc))
                if x2_stored:
                    ad_w.append((sOUT, x2_stored))
                ad = op("vector", (lambda e, qt=qt: e.tensor_add(x2_b, ln_b, x1r[qt % 2])),
                        sV, waits=ad_w)
                prev_norm = ad
                x1_slot[qt % 2] = ad
                x2_stored = dma("sync", out_d[b * BLK + qt * 128: b * BLK + (qt + 1) * 128, :], x2_b,
                                sOUT, waits=[(sV, ad)])
                if qt == 2 and b + 1 < NBLK:
                    emit_tchain(b + 1, [])
            b_ln_prev = prev_ln2_sc

        ev("sync", [(sOUT, sOUT.n)], lambda e: e.nop())

        with nc.Block() as block:
            def runner(name):
                def _run(e):
                    for (eng, waits, fn) in events:
                        if eng != name:
                            continue
                        for (s, v) in waits:
                            e.wait_ge(s.h, v)
                        fn(e)
                return _run

            block.sync(runner("sync"))
            block.tensor(runner("tensor"))
            block.scalar(runner("scalar"))
            block.vector(runner("vector"))
            block.gpsimd(runner("gpsimd"))

    return nc


_NC_CACHE = {}


def _get_nc():
    if "nc" not in _NC_CACHE:
        _NC_CACHE["nc"] = build_program()
    return _NC_CACHE["nc"]


def kernel(L0_attn_x, L0_attn_w_qkv, L0_attn_w_o, L0_ffn_fast_b, L0_ffn_w_fc,
           L0_ffn_w_proj, L0_ffn_u, L0_ffn_v, L0_ffn_gate):
    x = np.ascontiguousarray(np.asarray(L0_attn_x, dtype=np.float32))
    wqkvT = np.ascontiguousarray(np.asarray(L0_attn_w_qkv, np.float32).T)
    wo_t = np.asarray(L0_attn_w_o, np.float32).T
    woq = np.empty((64, 16384), np.float32)
    for nh in range(2):
        for h in range(16):
            woq[:, nh * 8192 + h * 512: nh * 8192 + (h + 1) * 512] = \
                wo_t[h * 64:(h + 1) * 64, nh * 512:(nh + 1) * 512]
    woq = np.ascontiguousarray(woq)
    wfc = np.ascontiguousarray(np.asarray(L0_ffn_w_fc, np.float32))
    wproj = np.ascontiguousarray(np.asarray(L0_ffn_w_proj, np.float32))
    vT = np.ascontiguousarray(np.asarray(L0_ffn_v, np.float32).T)
    fbT = np.ascontiguousarray(np.asarray(L0_ffn_fast_b, np.float32).T)
    uT = np.ascontiguousarray(np.asarray(L0_ffn_u, np.float32).T)
    gate = np.ascontiguousarray(np.asarray(L0_ffn_gate, np.float32).reshape(1, 1))

    nc = _get_nc()
    shared = {"wqkvT": wqkvT, "woq": woq, "wfc": wfc, "wproj": wproj,
              "vT": vT, "fbT": fbT, "uT": uT, "gate": gate}
    in_maps = [dict(shared, x0=np.ascontiguousarray(x[c])) for c in range(8)]
    res = run_bass_kernel_spmd(nc, in_maps, core_ids=list(range(8)))
    return np.stack([np.asarray(r["out"], np.float32) for r in res.results], axis=0)


if __name__ == "__main__":
    nc = build_program()
    n = 0
    for _ in nc.all_instructions():
        n += 1
    print("instructions:", n)
